# version 03
# baseline (speedup 1.0000x reference)
from contextlib import ExitStack

import numpy as np
import concourse.bass as bass
import concourse.mybir as mybir
from concourse.bass_utils import run_bass_kernel_spmd

F32 = mybir.dt.float32
BF16 = mybir.dt.bfloat16
AF = mybir.ActivationFunctionType
ALU = mybir.AluOpType
AX = mybir.AxisListType

D = 2048
T = 1024
NT = 16
NE = 32
NI = 6
NCORES = 8
EPS = 1e-5

SEM_KEYS = ("MM", "EVS", "EVV", "TP", "GV", "GS", "CC", "ST", "VC",
            "WC", "WX", "XR", "XS", "KVB", "KSR", "WR0", "WR1", "OH0", "OH1")


class Ctx:

    def __init__(self, sems):
        self.sems = sems
        self.c = {k: 0 for k in SEM_KEYS}

    def inc(self, inst, key, n=1):
        inst.then_inc(self.sems[key], n)
        self.c[key] += n
        return self.c[key]

    def wait(self, eng, *specs):
        best = {}
        for spec in specs:
            if callable(spec):
                spec = spec()
            if not spec:
                continue
            key, val = spec
            if val and val > 0:
                best[key] = max(best.get(key, 0), val)
        for key, val in best.items():
            eng.wait_ge(self.sems[key], val)


class Pipeline:

    def __init__(self, cx, psums, barrier=()):
        self.cx = cx
        self.psums = psums
        self.barrier = list(barrier)
        self.entries = []
        self.fills = []

    def add(self, key, mm, evac_eng, evac, pe_waits=(), evac_waits=(),
            post=None, post_waits=(), tag=None):
        f = dict(kind="fill", key=key, mm=mm, eng=evac_eng, evac=evac,
                 pe_waits=list(pe_waits), evac_waits=list(evac_waits),
                 post=post, post_waits=list(post_waits), tag=tag)
        self.entries.append(f)
        self.fills.append(f)
        return f

    def add_custom(self, pe=None, v=None, s=None, mm_incs=0):
        self.entries.append(dict(kind="custom", pe=pe, v=v, s=s,
                                 mm_incs=mm_incs))

    def plan(self):
        cx = self.cx
        evs, evv, mm = cx.c["EVS"], cx.c["EVV"], cx.c["MM"]
        last_on = {}
        for f in self.entries:
            if f["kind"] == "custom":
                mm += f["mm_incs"]
                continue
            mm += 1
            f["mm_ms"] = mm
            if f["eng"] == "S":
                evs += 1
                f["evac_ms"] = ("EVS", evs)
            else:
                evv += 1
                f["evac_ms"] = ("EVV", evv)
            if f["post"] is not None:
                evv += 1
                f["post_ms"] = ("EVV", evv)
            f["prev_same"] = last_on.get(f["key"])
            last_on[f["key"]] = f
        self.planned = dict(MM=mm, EVS=evs, EVV=evv)
        return self

    def emit(self, blk):
        cx, psums = self.cx, self.psums

        def tbody(tensor):
            for f in self.entries:
                if f["kind"] == "custom":
                    if f["pe"]:
                        f["pe"](tensor)
                    continue
                waits = list(f["pe_waits"])
                p = f["prev_same"]
                if p is None:
                    waits += self.barrier
                else:
                    waits.append(p["evac_ms"])
                    if p["post"] is not None:
                        waits.append(p["post_ms"])
                cx.wait(tensor, *waits)
                inst = f["mm"](tensor, psums[f["key"]])
                got = cx.inc(inst, "MM")
                assert got == f["mm_ms"], (got, f["mm_ms"], f["tag"])

        def sbody(scalar):
            for f in self.entries:
                if f["kind"] == "custom":
                    if f["s"]:
                        f["s"](scalar)
                    continue
                if f["eng"] != "S":
                    continue
                cx.wait(scalar, ("MM", f["mm_ms"]), *f["evac_waits"])
                inst = f["evac"](scalar, psums[f["key"]])
                got = cx.inc(inst, "EVS")
                assert ("EVS", got) == f["evac_ms"], f["tag"]

        def vbody(vector):
            for f in self.entries:
                if f["kind"] == "custom":
                    if f["v"]:
                        f["v"](vector)
                    continue
                if f["eng"] == "V":
                    cx.wait(vector, ("MM", f["mm_ms"]), *f["evac_waits"])
                    inst = f["evac"](vector, psums[f["key"]])
                    got = cx.inc(inst, "EVV")
                    assert ("EVV", got) == f["evac_ms"], f["tag"]
                if f["post"] is not None:
                    cx.wait(vector, f["evac_ms"], *f["post_waits"])
                    inst = f["post"](vector)
                    got = cx.inc(inst, "EVV")
                    assert got == f["post_ms"][1], f["tag"]

        blk.tensor(tbody)
        blk.scalar(sbody)
        blk.vector(vbody)
        for k, v in self.planned.items():
            assert cx.c[k] == v, (k, cx.c[k], v)
        return [("MM", self.planned["MM"]), ("EVS", self.planned["EVS"]),
                ("EVV", self.planned["EVV"])]


def build_nc():
    nc = bass.Bass("TRN2", target_bir_lowering=False, debug=False)

    xT = nc.dram_tensor("xT", [NT, 128, T], F32, kind="ExternalInput").ap()
    wq = nc.dram_tensor("wq", [4, 128, 8192], F32, kind="ExternalInput").ap()
    wk = nc.dram_tensor("wk", [4, 128, 8192], F32, kind="ExternalInput").ap()
    wv = nc.dram_tensor("wv", [4, 128, 8192], F32, kind="ExternalInput").ap()
    wo = nc.dram_tensor("wo", [4, 128, 8192], F32, kind="ExternalInput").ap()
    boc = nc.dram_tensor("boc", [128, 16], F32, kind="ExternalInput").ap()
    gatep = nc.dram_tensor("gatep", [128, 512], F32, kind="ExternalInput").ap()
    gup = nc.dram_tensor("gup", [NE, 3, 128, 8192], F32,
                         kind="ExternalInput").ap()
    dwn = nc.dram_tensor("dwn", [NE, 2, 128, 6144], F32,
                         kind="ExternalInput").ap()
    cid = nc.dram_tensor("cid", [128, 128], F32, kind="ExternalInput").ap()
    cpat = nc.dram_tensor("cpat", [128, 2], F32, kind="ExternalInput").ap()
    cpatT = nc.dram_tensor("cpatT", [2, 128], F32, kind="ExternalInput").ap()
    coh = nc.dram_tensor("coh", [32, 4096], F32, kind="ExternalInput").ap()
    conec = nc.dram_tensor("conec", [128, 1], F32, kind="ExternalInput").ap()
    coner = nc.dram_tensor("coner", [1, 128], F32, kind="ExternalInput").ap()
    out = nc.dram_tensor("out", [D, T], F32, kind="ExternalOutput").ap()
    cc_in = nc.dram_tensor("cc_in", [128, 1040], F32)
    cc_out = nc.dram_tensor("cc_out", [128, 1040], F32)

    with ExitStack() as ctx:
        E = ctx.enter_context
        AR_RES = E(nc.sbuf_tensor("AR_RES", [128, NT * T], F32))
        AR_H = E(nc.sbuf_tensor("AR_H", [128, NT * T], BF16))
        AR_Q = E(nc.sbuf_tensor("AR_Q", [128, NT * T], BF16))
        AR_W = E(nc.sbuf_tensor("AR_W", [128, 16384], BF16))
        SCR = E(nc.sbuf_tensor("SCR", [128, 2048], F32))
        SROW_A = E(nc.sbuf_tensor("SROW_A", [1, T], F32))
        SROW_B = E(nc.sbuf_tensor("SROW_B", [1, T], F32))
        n_sb = E(nc.sbuf_tensor("n_sb", [2, T], F32))
        r_f = E(nc.sbuf_tensor("r_f", [2, T], F32))
        r_bf = E(nc.sbuf_tensor("r_bf", [2, T], BF16))
        qk_t = E(nc.sbuf_tensor("qk_t", [128, T], BF16))
        kv_bf = E(nc.sbuf_tensor("kv_bf", [128, 1024], BF16))
        keysum = E(nc.sbuf_tensor("keysum", [128, 16], F32))
        probs_tm = E(nc.sbuf_tensor("probs_tm", [128, 256], F32))
        e_t = E(nc.sbuf_tensor("e_t", [128, 256], F32))
        rw1 = E(nc.sbuf_tensor("rw1", [128, 256], F32))
        rw_tm = E(nc.sbuf_tensor("rw_tm", [128, 256], F32))
        rw_em_bf = E(nc.sbuf_tensor("rw_em_bf", [32, 1024], BF16))
        top8 = E(nc.sbuf_tensor("top8", [128, 8], F32))
        scol = E(nc.sbuf_tensor("scol", [128, 8], F32))
        rcol = E(nc.sbuf_tensor("rcol", [128, 8], F32))
        s4col = E(nc.sbuf_tensor("s4col", [128, 8], F32))
        r4col = E(nc.sbuf_tensor("r4col", [128, 8], F32))
        rwb_bf = E(nc.sbuf_tensor("rwb_bf", [128, 2048], BF16))
        gate_bf = E(nc.sbuf_tensor("gate_bf", [128, 512], BF16))
        bo_sb = E(nc.sbuf_tensor("bo_sb", [128, 16], F32))
        idf = E(nc.sbuf_tensor("idf", [128, 128], F32))
        idb = E(nc.sbuf_tensor("idb", [128, 128], BF16))
        pat_bf = E(nc.sbuf_tensor("pat_bf", [128, 2], BF16))
        patT_bf = E(nc.sbuf_tensor("patT_bf", [2, 128], BF16))
        oh_sb = E(nc.sbuf_tensor("oh_sb", [32, 256], BF16))
        onec_bf = E(nc.sbuf_tensor("onec_bf", [128, 1], BF16))
        oner_f = E(nc.sbuf_tensor("oner_f", [1, 128], F32))
        PS_A = E(nc.psum_tensor("PS_A", [128, 1024], F32))
        PS_B = E(nc.psum_tensor("PS_B", [128, 1024], F32))
        PS_KV = E(nc.psum_tensor("PS_KV", [128, 1024], F32))
        PS_TP = E(nc.psum_tensor("PS_TP", [128, 1024], BF16))
        PS_SP = E(nc.psum_tensor("PS_SP", [128, 512], F32))
        sems = {k: E(nc.semaphore("s" + k)) for k in SEM_KEYS}
        blk = E(nc.Block())

        cx = Ctx(sems)
        psums = {"A": PS_A, "B": PS_B, "KV": PS_KV}
        arb = AR_RES.ap().bitcast(BF16)
        tpf = PS_TP.ap().bitcast(F32)

        def h_t(i):
            return AR_H[:, T * i:T * (i + 1)]

        def q_t(i):
            return AR_Q[:, T * i:T * (i + 1)]

        def res_t(i):
            return AR_RES[:, T * i:T * (i + 1)]

        def kq_t(dt):
            return arb[:, T * dt:T * (dt + 1)]

        def vq_t(dt):
            return arb[:, 4096 + T * dt:4096 + T * (dt + 1)]

        ktp = arb[:, 8192:9216]
        vtp = arb[:, 9216:10240]

        wplan = []
        wms = {}
        dcount = {k: 0 for k in SEM_KEYS}

        def plan_dma(name, key):
            dcount[key] += 16
            wms[name] = (key, dcount[key])
            return wms[name]

        def add_plain(name, dst, src, key):
            wplan.append((name, "plain", (dst, src), key))
            plan_dma(name, key)

        ring_entries = []

        def slot_ap(idx, width):
            base = 8192 * (idx % 2)
            return AR_W[:, base:base + width]

        def add_ring(name, src, width=8192):
            idx = len(ring_entries)
            ring_entries.append(name)
            key = "WR0" if idx % 2 == 0 else "WR1"
            wplan.append((name, "ring", (idx, width, src), key))
            plan_dma(name, key)
            return idx

        for nm, dst, src in [("idb", idb[:, :], cid),
                             ("pat_bf", pat_bf[:, :], cpat),
                             ("patT_bf", patT_bf[:, :], cpatT),
                             ("gate_bf", gate_bf[:, :], gatep),
                             ("onec_bf", onec_bf[:, :], conec),
                             ("idf", idf[:, :], cid),
                             ("bo_sb", bo_sb[:, :], boc),
                             ("oner_f", oner_f[:, :], coner)]:
            add_plain(nm, dst, src, "WC")
        for i in range(NT):
            add_plain(f"x{i}", h_t(i), xT[i, :, :], "WX")
        WC_DONE = ("WC", dcount["WC"])
        WX_DONE = ("WX", dcount["WX"])
        ring_idx = {}
        for q in range(4):
            ring_idx[f"wk{q}"] = add_ring(f"wk{q}", wk[q, :, :])
            ring_idx[f"wv{q}"] = add_ring(f"wv{q}", wv[q, :, :])
        for q in range(2):
            ring_idx[f"wq{q}"] = add_ring(f"wq{q}", wq[q, :, :])
        wplan.append(("__collective__", "cc", None, None))
        add_plain("kv_bf", kv_bf[:, :], cc_out[:, 0:1024], "KVB")
        add_plain("keysum_r", keysum[:, :], cc_out[:, 1024:1040], "KSR")
        for q in range(2, 4):
            ring_idx[f"wq{q}"] = add_ring(f"wq{q}", wq[q, :, :])
        for q in range(4):
            ring_idx[f"wo{q}"] = add_ring(f"wo{q}", wo[q, :, :])
        for e in range(NE):
            key = "OH0" if e % 2 == 0 else "OH1"
            wplan.append((f"oh{e}", "plainw",
                          (oh_sb[:, 128 * (e % 2):128 * (e % 2) + 128],
                           coh[:, 128 * e:128 * (e + 1)], e), key))
            plan_dma(f"oh{e}", key)
            for t in range(3):
                ring_idx[f"g{e}_{t}"] = add_ring(f"g{e}_{t}", gup[e, t, :, :])
            for hf in range(2):
                ring_idx[f"d{e}_{hf}"] = add_ring(f"d{e}_{hf}",
                                                  dwn[e, hf, :, :], width=6144)
        ring_consumed = {}

        def proj_fill_mm(wname, dt, src_tile_fn):
            def mm(tensor, psum):
                for s in range(NT):
                    lo = 512 * s + 128 * dt
                    st, sp = (s == 0), (s == NT - 1)
                    for hh in range(2):
                        c = slice(512 * hh, 512 * (hh + 1))
                        inst = tensor.matmul(
                            psum[:, c],
                            slot_ap(ring_idx[wname], 8192)[:, lo:lo + 128],
                            src_tile_fn(s)[:, c], start=st, stop=sp)
                return inst
            return mm

        g_sq = []

        def v0a(vector):
            cx.wait(vector, WX_DONE)
            for i in range(NT):
                inst = vector.tensor_mul(q_t(i), h_t(i), h_t(i))
                g_sq.append(cx.inc(inst, "GV"))
        blk.vector(v0a)

        def ln_stats_and_h(sq_ready):
            ms = {}

            def t_stats(tensor):
                cx.wait(tensor, WC_DONE)
                for i in range(NT):
                    cx.wait(tensor, ("GV", sq_ready[i]))
                    st, sp = (i == 0), (i == NT - 1)
                    for hh in range(2):
                        c = slice(512 * hh, 512 * (hh + 1))
                        tensor.matmul(PS_A[0:1, c], onec_bf[:, :],
                                      h_t(i)[:, c], start=st, stop=sp)
                        inst = tensor.matmul(PS_B[0:1, c], onec_bf[:, :],
                                             q_t(i)[:, c], start=st, stop=sp)
                ms["stats"] = cx.inc(inst, "MM")
            blk.tensor(t_stats)

            def v_rows(vector):
                cx.wait(vector, ("MM", ms["stats"]))

                def step(inst):
                    c = cx.inc(inst, "VC")
                    vector.wait_ge(sems["VC"], c)
                step(vector.tensor_copy(SROW_A[:, :], PS_A[0:1, :]))
                step(vector.tensor_copy(SROW_B[:, :], PS_B[0:1, :]))
                step(vector.tensor_scalar_mul(SROW_A[:, :], SROW_A[:, :],
                                              1.0 / D))
                step(vector.tensor_scalar_mul(SROW_B[:, :], SROW_B[:, :],
                                              1.0 / D))
                step(vector.tensor_mul(n_sb[0:1, :], SROW_A[:, :],
                                       SROW_A[:, :]))
                step(vector.tensor_sub(SROW_B[:, :], SROW_B[:, :],
                                       n_sb[0:1, :]))
                inst = vector.tensor_scalar_add(SROW_B[:, :], SROW_B[:, :],
                                                EPS)
                ms["var"] = cx.inc(inst, "GV")
            blk.vector(v_rows)

            def s_rows(scalar):
                cx.wait(scalar, ("GV", ms["var"]))
                inst = scalar.activation(n_sb[0:1, :], SROW_B[:, :], AF.Ln)
                m1 = cx.inc(inst, "GS")
                cx.wait(scalar, ("GS", m1))
                inst = scalar.activation(SROW_B[:, :], n_sb[0:1, :], AF.Exp,
                                         scale=-0.5)
                ms["s"] = cx.inc(inst, "GS")
            blk.scalar(s_rows)

            def v_msrow(vector):
                cx.wait(vector, ("GS", ms["s"]))
                inst = vector.tensor_mul(SROW_A[:, :], SROW_A[:, :],
                                         SROW_B[:, :])
                ms["msrow"] = cx.inc(inst, "GV")
            blk.vector(v_msrow)

            def t_bcast(tensor):
                cx.wait(tensor, ("GV", ms["msrow"]), WC_DONE)
                for hh in range(2):
                    c = slice(512 * hh, 512 * (hh + 1))
                    tensor.matmul(PS_A[:, c], oner_f[:, :], SROW_B[0:1, c],
                                  start=True, stop=True)
                    inst = tensor.matmul(PS_B[:, c], oner_f[:, :],
                                         SROW_A[0:1, c], start=True, stop=True)
                ms["bc"] = cx.inc(inst, "MM")
            blk.tensor(t_bcast)

            h_done = []

            def v_apply(vector):
                cx.wait(vector, ("MM", ms["bc"]))
                for i in range(NT):
                    c = cx.inc(vector.tensor_mul(h_t(i), h_t(i), PS_A[:, :]),
                               "VC")
                    vector.wait_ge(sems["VC"], c)
                    inst = vector.tensor_sub(h_t(i), h_t(i), PS_B[:, :])
                    h_done.append(cx.inc(inst, "GV"))
                ms["bc_ev"] = h_done[-1]
            blk.vector(v_apply)
            return ms, h_done

        ms0, h_done = ln_stats_and_h(g_sq)

        ph1_barrier = [("GV", h_done[-1]), ("GV", ms0["bc_ev"])]
        pipe1 = Pipeline(cx, psums, barrier=ph1_barrier)
        tp0 = cx.c["TP"]
        kv_ms = {}
        kfills, vfills = [], []
        for q in range(4):
            for dt in range(4):
                gi = 4 * q + dt

                def k_evac(scalar, psum, dt=dt):
                    return scalar.activation(kq_t(dt), psum[:, :], AF.Relu)

                def k_post(vector, gi=gi, dt=dt):
                    return vector.reduce_sum(keysum[:, gi:gi + 1], kq_t(dt),
                                             axis=AX.X)
                kfills.append(pipe1.add(
                    "A" if gi % 2 == 0 else "B", proj_fill_mm(f"wk{q}", dt, h_t),
                    "S", k_evac, pe_waits=[wms[f"wk{q}"]],
                    post=k_post, tag=f"k{gi}"))
            for dt in range(4):
                gi = 4 * q + dt

                def v_evac(scalar, psum, dt=dt):
                    return scalar.activation(vq_t(dt), psum[:, :], AF.Identity)
                vfills.append(pipe1.add(
                    "A" if gi % 2 == 0 else "B", proj_fill_mm(f"wv{q}", dt, h_t),
                    "S", v_evac, pe_waits=[wms[f"wv{q}"]], tag=f"v{gi}"))

            def tkv_pe(tensor, q=q):
                if q == 0:
                    cx.wait(tensor, WC_DONE)
                for dt in range(4):
                    gi = 4 * q + dt
                    cx.wait(tensor, ("TP", tp0 + 2 * gi),
                            kfills[gi]["evac_ms"])
                    for b in range(8):
                        inst = tensor.transpose(
                            PS_TP[:, 128 * b:128 * (b + 1)],
                            kq_t(dt)[:, 128 * b:128 * (b + 1)], idb[:, :])
                    kv_ms[f"ktr{gi}"] = cx.inc(inst, "MM")
                    cx.wait(tensor, ("TP", tp0 + 2 * gi + 1),
                            vfills[gi]["evac_ms"])
                    for b in range(8):
                        inst = tensor.transpose(
                            PS_TP[:, 128 * b:128 * (b + 1)],
                            vq_t(dt)[:, 128 * b:128 * (b + 1)], idb[:, :])
                    kv_ms[f"vtr{gi}"] = cx.inc(inst, "MM")
                    cx.wait(tensor, ("TP", tp0 + 2 * gi + 2))
                    co = slice(64 * gi, 64 * gi + 64)
                    for b in range(8):
                        tensor.matmul(PS_KV[0:64, co],
                                      ktp[:, 128 * b:128 * b + 64],
                                      vtp[:, 128 * b:128 * b + 64],
                                      start=(b == 0), stop=(b == 7))
                    for b in range(8):
                        inst = tensor.matmul(
                            PS_KV[64:128, co],
                            ktp[:, 128 * b + 64:128 * b + 128],
                            vtp[:, 128 * b + 64:128 * b + 128],
                            start=(b == 0), stop=(b == 7))
                    kv_ms[f"kvmm{gi}"] = cx.inc(inst, "MM")

            def tkv_v(vector, q=q):
                for dt in range(4):
                    gi = 4 * q + dt
                    cx.wait(vector, ("MM", kv_ms[f"ktr{gi}"]))
                    inst = vector.tensor_copy(ktp, PS_TP[:, :])
                    got = cx.inc(inst, "TP")
                    assert got == tp0 + 2 * gi + 1
                    cx.wait(vector, ("MM", kv_ms[f"vtr{gi}"]))
                    inst = vector.tensor_copy(vtp, PS_TP[:, :])
                    got = cx.inc(inst, "TP")
                    assert got == tp0 + 2 * gi + 2
            pipe1.add_custom(pe=tkv_pe, v=tkv_v, mm_incs=12)
        pipe1.plan()
        for q in range(4):
            ring_consumed[f"wk{q}"] = kfills[4 * q + 3]["mm_ms"]
            ring_consumed[f"wv{q}"] = vfills[4 * q + 3]["mm_ms"]
        p1_end = pipe1.emit(blk)
        kv_done_planned = kv_ms["kvmm15"]

        ms_kvsb = {}

        def v_kvsb(vector):
            cx.wait(vector, ("MM", kv_ms["kvmm15"]))
            inst = vector.tensor_copy(SCR[:, 0:1024], PS_KV[:, :])
            ms_kvsb["v"] = cx.inc(inst, "GV")
        blk.vector(v_kvsb)

        def sync2(sync):
            cx.wait(sync, ("GV", ms_kvsb["v"]))
            inst = sync.dma_start(out=cc_in[:, 0:1024], in_=SCR[:, 0:1024])
            cx.inc(inst, "XS", 16)
            inst = sync.dma_start(out=cc_in[:, 1024:1040], in_=keysum[:, :])
            cx.inc(inst, "XS", 16)
            for i in range(NT):
                inst = sync.dma_start(out=res_t(i), in_=xT[i, :, :])
                cx.inc(inst, "XR", 16)
        blk.sync(sync2)
        XS_DONE = ("XS", cx.c["XS"])
        XR_DONE = ("XR", cx.c["XR"])

        pipe2 = Pipeline(cx, psums, barrier=p1_end)
        qfills = []
        for gi in range(NT):
            q, dt = gi // 4, gi % 4

            def q_evac(scalar, psum, gi=gi):
                return scalar.activation(q_t(gi), psum[:, :], AF.Relu)
            qfills.append(pipe2.add(
                "A" if gi % 2 == 0 else "B", proj_fill_mm(f"wq{q}", dt, h_t),
                "S", q_evac, pe_waits=[wms[f"wq{q}"]], tag=f"q{gi}"))
        pipe2.plan()
        for q in range(4):
            ring_consumed[f"wq{q}"] = qfills[4 * q + 3]["mm_ms"]
        p2_end = pipe2.emit(blk)

        evv30 = cx.c["EVV"]
        mm30 = cx.c["MM"]

        def vp(t, k):
            return ("EVV", evv30 + 8 * t + k)

        def pm(t, k):
            return ("MM", mm30 + 3 * t + k)

        def t3(tensor):
            cx.wait(tensor, WC_DONE, wms["kv_bf"], *p2_end)
            for t in range(NT):
                cx.wait(tensor, vp(t, 1))
                if t > 0:
                    cx.wait(tensor, vp(t - 1, 3))
                tensor.matmul(tpf[0:2, 0:512], pat_bf[:, :], qk_t[:, 0:512],
                              start=True, stop=True)
                inst = tensor.matmul(PS_SP[0:2, 0:512], pat_bf[:, :],
                                     qk_t[:, 512:1024], start=True, stop=True)
                cx.inc(inst, "MM")
                cx.wait(tensor, vp(t, 6))
                if t > 0:
                    cx.wait(tensor, vp(t - 1, 7))
                for hh in range(2):
                    c = slice(512 * hh, 512 * (hh + 1))
                    inst = tensor.matmul(PS_A[:, c], patT_bf[:, :],
                                         r_bf[:, c], start=True, stop=True)
                cx.inc(inst, "MM")
                if t > 1:
                    cx.wait(tensor, vp(t - 2, 8))
                ps = PS_KV if t % 2 == 0 else PS_B
                co = slice(64 * t, 64 * t + 64)
                for hh in range(2):
                    c = slice(512 * hh, 512 * (hh + 1))
                    tensor.matmul(ps[0:64, c], kv_bf[0:64, co],
                                  q_t(t)[0:64, c], start=True, stop=True)
                    inst = tensor.matmul(ps[64:128, c], kv_bf[64:128, co],
                                         q_t(t)[64:128, c], start=True,
                                         stop=True)
                cx.inc(inst, "MM")
        blk.tensor(t3)

        def v3(vector):
            cx.wait(vector, wms["keysum_r"])
            for t in range(NT):
                cx.wait(vector, qfills[t]["evac_ms"])
                if t > 0:
                    cx.wait(vector, pm(t - 1, 1))
                inst = vector.tensor_scalar_mul(qk_t[:, :], q_t(t),
                                                keysum[:, t:t + 1])
                cx.inc(inst, "EVV")
                cx.wait(vector, pm(t, 1))
                inst = vector.tensor_copy(n_sb[0:2, 0:512], tpf[0:2, 0:512])
                cx.inc(inst, "EVV")
                inst = vector.tensor_copy(n_sb[0:2, 512:1024],
                                          PS_SP[0:2, 0:512])
                cx.inc(inst, "EVV")
                cx.wait(vector, vp(t, 3))
                inst = vector.tensor_scalar_max(n_sb[0:2, :], n_sb[0:2, :],
                                                1e-6)
                cx.inc(inst, "EVV")
                cx.wait(vector, vp(t, 4))
                inst = vector.reciprocal(r_f[0:2, :], n_sb[0:2, :])
                cx.inc(inst, "EVV")
                cx.wait(vector, vp(t, 5))
                inst = vector.tensor_copy(r_bf[0:2, :], r_f[0:2, :])
                cx.inc(inst, "EVV")
                cx.wait(vector, pm(t, 2))
                rb_dst = SCR[:, 0:1024] if t % 2 == 0 else SCR[:, 1024:2048]
                inst = vector.tensor_copy(rb_dst, PS_A[:, :])
                cx.inc(inst, "EVV")
                cx.wait(vector, pm(t, 3), vp(t, 7))
                ps = PS_KV if t % 2 == 0 else PS_B
                inst = vector.tensor_mul(q_t(t), ps[:, :], rb_dst)
                cx.inc(inst, "EVV")
        blk.vector(v3)
        assert cx.c["EVV"] == evv30 + 8 * NT
        assert cx.c["MM"] == mm30 + 3 * NT
        p3_end = [("EVV", cx.c["EVV"]), ("MM", cx.c["MM"]),
                  ("EVS", cx.c["EVS"])]

        pipe4 = Pipeline(cx, psums, barrier=p3_end)
        wofills = []
        for gi in range(NT):
            q, dt = gi // 4, gi % 4

            def wo_evac(scalar, psum, gi=gi):
                tdst = SCR[:, 0:1024] if gi % 2 == 0 else SCR[:, 1024:2048]
                return scalar.activation(tdst, psum[:, :], AF.Identity,
                                         bias=bo_sb[:, gi:gi + 1])

            def wo_post(vector, gi=gi):
                tsrc = SCR[:, 0:1024] if gi % 2 == 0 else SCR[:, 1024:2048]
                return vector.tensor_add(res_t(gi), res_t(gi), tsrc)
            wofills.append(pipe4.add(
                "A" if gi % 2 == 0 else "B", proj_fill_mm(f"wo{q}", dt, q_t),
                "S", wo_evac,
                pe_waits=[wms[f"wo{q}"], WC_DONE],
                post=wo_post, post_waits=[XR_DONE],
                tag=f"wo{gi}"))
        pipe4.plan()
        for gi in range(2, NT):
            wofills[gi]["evac_waits"].append(wofills[gi - 2]["post_ms"])
        for q in range(4):
            ring_consumed[f"wo{q}"] = wofills[4 * q + 3]["mm_ms"]
        p4_end = pipe4.emit(blk)

        g_sq2 = []

        def v5a(vector):
            cx.wait(vector, *p4_end)
            for i in range(NT):
                c = cx.inc(vector.tensor_copy(h_t(i), res_t(i)), "VC")
                vector.wait_ge(sems["VC"], c)
                inst = vector.tensor_mul(q_t(i), h_t(i), h_t(i))
                g_sq2.append(cx.inc(inst, "GV"))
        blk.vector(v5a)
        ms5, h2_done = ln_stats_and_h(g_sq2)

        ms6 = {}

        def t6a(tensor):
            cx.wait(tensor, WC_DONE, ("GV", h2_done[-1]),
                    ("GV", ms5["bc_ev"]))
            for hh in range(2):
                c = slice(512 * hh, 512 * (hh + 1))
                for s in range(NT):
                    inst = tensor.matmul(PS_KV[0:32, c],
                                         gate_bf[:, 32 * s:32 * (s + 1)],
                                         h_t(s)[:, c],
                                         start=(s == 0), stop=(s == NT - 1))
            ms6["logits"] = cx.inc(inst, "MM")
        blk.tensor(t6a)

        def v6a(vector):
            cx.wait(vector, ("MM", ms6["logits"]))
            inst = vector.tensor_copy(SCR[0:32, 0:1024], PS_KV[0:32, :])
            ms6["lg"] = cx.inc(inst, "GV")
        blk.vector(v6a)

        for pair in range(4):
            def t6b(tensor, pair=pair):
                w = [("GV", ms6["lg"]), WC_DONE]
                if pair > 0:
                    w.append(("GV", ms6[f"trv{pair - 1}"]))
                cx.wait(tensor, *w)
                for j in range(2):
                    b = 2 * pair + j
                    inst = tensor.transpose(
                        tpf[:, 32 * j:32 * (j + 1)],
                        SCR[0:32, 128 * b:128 * (b + 1)], idf[0:32, 0:32])
                ms6[f"tr{pair}"] = cx.inc(inst, "MM")
            blk.tensor(t6b)

            def v6b(vector, pair=pair):
                cx.wait(vector, ("MM", ms6[f"tr{pair}"]))
                for j in range(2):
                    b = 2 * pair + j
                    inst = vector.tensor_copy(
                        probs_tm[:, 32 * b:32 * (b + 1)],
                        tpf[:, 32 * j:32 * (j + 1)])
                ms6[f"trv{pair}"] = cx.inc(inst, "GV")
            blk.vector(v6b)

        def s6(scalar):
            for b in range(8):
                cx.wait(scalar, ("GV", ms6[f"trv{b // 2}"]))
                inst = scalar.activation(e_t[:, 32 * b:32 * (b + 1)],
                                         probs_tm[:, 32 * b:32 * (b + 1)],
                                         AF.Exp)
                ms6[f"exp{b}"] = cx.inc(inst, "GS")
        blk.scalar(s6)

        def v6c(vector):
            def step(inst):
                c = cx.inc(inst, "VC")
                vector.wait_ge(sems["VC"], c)
            for b in range(8):
                cx.wait(vector, ("GS", ms6[f"exp{b}"]))
                eb = e_t[:, 32 * b:32 * (b + 1)]
                step(vector.reduce_sum(scol[:, b:b + 1], eb, axis=AX.X))
                step(vector.reciprocal(rcol[:, b:b + 1], scol[:, b:b + 1]))
                pb = probs_tm[:, 32 * b:32 * (b + 1)]
                step(vector.tensor_scalar_mul(pb, eb, rcol[:, b:b + 1]))
                step(vector.max(out=top8[:, :], in_=pb))
                step(vector.reduce_sum(s4col[:, b:b + 1], top8[:, 0:4],
                                       axis=AX.X))
                step(vector.reciprocal(r4col[:, b:b + 1], s4col[:, b:b + 1]))
                rb1 = rw1[:, 32 * b:32 * (b + 1)]
                step(vector.tensor_scalar(rb1, pb, top8[:, 3:4], None,
                                          op0=ALU.is_ge))
                step(vector.tensor_mul(rb1, rb1, pb))
                inst = vector.tensor_scalar_mul(rw_tm[:, 32 * b:32 * (b + 1)],
                                                rb1, r4col[:, b:b + 1])
                ms6[f"rw{b}"] = cx.inc(inst, "GV")
        blk.vector(v6c)

        for pair in range(4):
            def t6c(tensor, pair=pair):
                w = [("GV", ms6[f"rw{2 * pair}"]),
                     ("GV", ms6[f"rw{2 * pair + 1}"])]
                w.append(("GV", ms6[f"rtv{pair - 1}"]) if pair > 0
                         else ("GV", ms6["trv3"]))
                cx.wait(tensor, *w)
                for j in range(2):
                    b = 2 * pair + j
                    inst = tensor.transpose(
                        tpf[0:32, 128 * j:128 * (j + 1)],
                        rw_tm[:, 32 * b:32 * (b + 1)], idf[:, :])
                ms6[f"rt{pair}"] = cx.inc(inst, "MM")
            blk.tensor(t6c)

            def v6d(vector, pair=pair):
                cx.wait(vector, ("MM", ms6[f"rt{pair}"]))
                for j in range(2):
                    b = 2 * pair + j
                    inst = vector.tensor_copy(
                        SCR[0:32, 1024 + 128 * b:1024 + 128 * (b + 1)],
                        tpf[0:32, 128 * j:128 * (j + 1)])
                ms6[f"rtv{pair}"] = cx.inc(inst, "GV")
            blk.vector(v6d)

        def v6e(vector):
            cx.wait(vector, ("GV", ms6["rtv3"]))
            inst = vector.tensor_copy(rw_em_bf[:, :], SCR[0:32, 1024:2048])
            ms6["rw_bf"] = cx.inc(inst, "GV")
        blk.vector(v6e)
        p6_end = [("GV", ms6["rw_bf"])]

        pipe7 = Pipeline(cx, psums,
                         barrier=p6_end + [("GV", h2_done[-1]),
                                           ("GV", ms5["bc_ev"])])
        gfills = {}
        ufills = {}
        dfills = {}
        rbfills = []

        def rb_mm(e):
            def mm(tensor, psum):
                for hh in range(2):
                    c = slice(512 * hh, 512 * (hh + 1))
                    inst = tensor.matmul(
                        psum[:, c],
                        oh_sb[:, 128 * (e % 2):128 * (e % 2) + 128],
                        rw_em_bf[:, c], start=True, stop=True)
                return inst
            return mm

        def gup_mm(e, t, dt):
            def mm(tensor, psum):
                wn = f"g{e}_{t}"
                for s in range(NT):
                    lo = 512 * s + 128 * dt
                    st, sp = (s == 0), (s == NT - 1)
                    for hh in range(2):
                        c = slice(512 * hh, 512 * (hh + 1))
                        inst = tensor.matmul(
                            psum[:, c],
                            slot_ap(ring_idx[wn], 8192)[:, lo:lo + 128],
                            h_t(s)[:, c], start=st, stop=sp)
                return inst
            return mm

        def dwn_mm(e, hf, dt):
            def mm(tensor, psum):
                wn = f"d{e}_{hf}"
                for s in range(NI):
                    lo = 1024 * s + 128 * dt
                    st, sp = (s == 0), (s == NI - 1)
                    for hh in range(2):
                        c = slice(512 * hh, 512 * (hh + 1))
                        inst = tensor.matmul(
                            psum[:, c],
                            slot_ap(ring_idx[wn], 6144)[:, lo:lo + 128],
                            q_t(6 + s)[:, c], start=st, stop=sp)
                return inst
            return mm

        nab = [0]

        def abkey():
            nab[0] += 1
            return "A" if nab[0] % 2 == 1 else "B"

        for e in range(NE):
            def rwb_evac(vector, psum, e=e):
                dst = rwb_bf[:, 1024 * (e % 2):1024 * (e % 2 + 1)]
                return vector.tensor_copy(dst, psum[:, :])
            rb_fill = pipe7.add("KV", rb_mm(e), "V", rwb_evac,
                                pe_waits=[wms[f"oh{e}"]],
                                tag=f"rb{e}")
            rbfills.append(rb_fill)
            for t in range(3):
                for dt in range(4):
                    j = 4 * t + dt
                    if j < 6:
                        def g_evac(scalar, psum, j=j):
                            return scalar.activation(q_t(j), psum[:, :],
                                                     AF.Silu)

                        def g_post(vector, j=j, e=e):
                            src = rwb_bf[:, 1024 * (e % 2):1024 * (e % 2 + 1)]
                            return vector.tensor_mul(q_t(j), q_t(j), src)
                        gfills[(e, j)] = pipe7.add(
                            abkey(), gup_mm(e, t, dt), "S", g_evac,
                            pe_waits=[wms[f"g{e}_{t}"]], post=g_post,
                            post_waits=[lambda f=rb_fill: f["evac_ms"]],
                            tag=f"g{e}_{j}")
                    else:
                        ju = j - 6

                        def u_evac(vector, psum, ju=ju):
                            return vector.tensor_mul(q_t(6 + ju), psum[:, :],
                                                     q_t(ju))
                        ufills[(e, ju)] = pipe7.add(
                            abkey(), gup_mm(e, t, dt), "V", u_evac,
                            pe_waits=[wms[f"g{e}_{t}"]],
                            evac_waits=[], tag=f"u{e}_{ju}")
            for hf in range(2):
                for dt in range(8):
                    gi = 8 * hf + dt

                    def d_evac(vector, psum, gi=gi):
                        return vector.tensor_add(res_t(gi), res_t(gi),
                                                 psum[:, :])
                    dfills[(e, gi)] = pipe7.add(
                        abkey(), dwn_mm(e, hf, dt), "V", d_evac,
                        pe_waits=[wms[f"d{e}_{hf}"]], tag=f"d{e}_{gi}")
        pipe7.plan()
        for e in range(NE):
            for ju in range(6):
                ufills[(e, ju)]["evac_waits"].append(
                    gfills[(e, ju)]["post_ms"])
            for gi in range(NT):
                dfills[(e, gi)]["pe_waits"].append(
                    ufills[(e, 5)]["evac_ms"])
            if e > 0:
                for j in range(6):
                    gfills[(e, j)]["evac_waits"].append(
                        ufills[(e - 1, j)]["evac_ms"])
            for t in range(3):
                last = [f for (ee, j), f in gfills.items() if ee == e] + \
                       [f for (ee, j), f in ufills.items() if ee == e]
            ring_consumed[f"g{e}_0"] = gfills[(e, 3)]["mm_ms"]
            ring_consumed[f"g{e}_1"] = ufills[(e, 1)]["mm_ms"]
            ring_consumed[f"g{e}_2"] = ufills[(e, 5)]["mm_ms"]
            ring_consumed[f"d{e}_0"] = dfills[(e, 7)]["mm_ms"]
            ring_consumed[f"d{e}_1"] = dfills[(e, 15)]["mm_ms"]
        pipe7.emit(blk)

        def sync_out(sync):
            for gi in range(NT):
                cx.wait(sync, dfills[(NE - 1, gi)]["evac_ms"])
                inst = sync.dma_start(out=out[128 * gi:128 * (gi + 1), :],
                                      in_=res_t(gi))
                cx.inc(inst, "ST", 16)
            cx.wait(sync, ("ST", cx.c["ST"]))
        blk.sync(sync_out)

        def gp(gpsimd):
            for name, kind, info, key in wplan:
                if kind == "cc":
                    cx.wait(gpsimd, XS_DONE)
                    inst = gpsimd.collective_compute(
                        "AllReduce", ALU.add,
                        replica_groups=[list(range(NCORES))],
                        ins=[cc_in[:, :].opt()], outs=[cc_out[:, :].opt()])
                    cx.inc(inst, "CC", 1)
                    gpsimd.wait_ge(sems["CC"], 1)
                    continue
                if kind == "plainw":
                    dst, src, e = info
                    if e >= 2:
                        cx.wait(gpsimd, ("MM", rbfills[e - 2]["mm_ms"]),
                                (key, 16 * (e // 2)))
                    inst = gpsimd.dma_start(out=dst, in_=src)
                elif kind == "plain":
                    dst, src = info
                    inst = gpsimd.dma_start(out=dst, in_=src)
                else:
                    idx, width, src = info
                    if idx >= 2:
                        prev = ring_entries[idx - 2]
                        cx.wait(gpsimd, ("MM", ring_consumed[prev]),
                                (key, 16 * (idx // 2)))
                    inst = gpsimd.dma_start(out=slot_ap(idx, width), in_=src)
                got = cx.inc(inst, key, 16)
                assert (key, got) == wms[name], (name, key, got, wms[name])
        blk.gpsimd(gp)

    return nc


def _pack_weights(inputs):
    f = np.float32

    def attn(w):
        return np.ascontiguousarray(
            np.asarray(w, f).reshape(16, 128, 4, 512).transpose(2, 1, 0, 3)
        ).reshape(4, 128, 8192)

    d = {}
    d["wq"] = attn(inputs["L0_attn_Wq"])
    d["wk"] = attn(inputs["L0_attn_Wk"])
    d["wv"] = attn(inputs["L0_attn_Wv"])
    d["wo"] = attn(inputs["L0_attn_Wo"])
    d["boc"] = np.ascontiguousarray(
        np.asarray(inputs["L0_attn_bo"], f).reshape(16, 128).T)
    d["gatep"] = np.ascontiguousarray(
        np.asarray(inputs["L0_ffn_gate"], f).reshape(16, 128, 32)
        .transpose(1, 0, 2)).reshape(128, 512)
    g = np.asarray(inputs["L0_ffn_gate_up_proj"], f)
    d["gup"] = np.ascontiguousarray(
        g.reshape(32, 16, 128, 3, 512).transpose(0, 3, 2, 1, 4)
    ).reshape(32, 3, 128, 8192)
    dn = np.asarray(inputs["L0_ffn_down_proj"], f)
    d["dwn"] = np.ascontiguousarray(
        dn.reshape(32, 6, 128, 2, 1024).transpose(0, 3, 2, 1, 4)
    ).reshape(32, 2, 128, 6144)
    d["cid"] = np.eye(128, dtype=f)
    pat = np.zeros((128, 2), f)
    pat[:64, 0] = 1.0
    pat[64:, 1] = 1.0
    d["cpat"] = pat
    d["cpatT"] = np.ascontiguousarray(pat.T)
    d["coh"] = np.ascontiguousarray(np.repeat(np.eye(32, dtype=f), 128,
                                              axis=1))
    d["conec"] = np.ones((128, 1), f)
    d["coner"] = np.ones((1, 128), f)
    return d


def kernel(L0_attn_hidden_states, L0_attn_Wq, L0_attn_Wk, L0_attn_Wv,
           L0_attn_Wo, L0_attn_bo, L0_ffn_gate, L0_ffn_gate_up_proj,
           L0_ffn_down_proj):
    shared = _pack_weights(dict(
        L0_attn_Wq=L0_attn_Wq, L0_attn_Wk=L0_attn_Wk, L0_attn_Wv=L0_attn_Wv,
        L0_attn_Wo=L0_attn_Wo, L0_attn_bo=L0_attn_bo, L0_ffn_gate=L0_ffn_gate,
        L0_ffn_gate_up_proj=L0_ffn_gate_up_proj,
        L0_ffn_down_proj=L0_ffn_down_proj))
    x = np.asarray(L0_attn_hidden_states, np.float32).reshape(8192, 2048)
    in_maps = []
    for c in range(NCORES):
        xc = np.ascontiguousarray(x[1024 * c:1024 * (c + 1), :].T)
        m = dict(shared)
        m["xT"] = xc.reshape(16, 128, 1024)
        in_maps.append(m)
    nc = build_nc()
    res = run_bass_kernel_spmd(nc, in_maps, core_ids=list(range(NCORES)))
    outp = np.empty((8192, 2048), np.float32)
    for c in range(NCORES):
        outp[1024 * c:1024 * (c + 1), :] = res.results[c]["out"].T
    return outp.reshape(1, 8192, 2048)


# version 04
# speedup vs baseline: 1.0077x; 1.0077x over previous
from contextlib import ExitStack

import numpy as np
import concourse.bass as bass
import concourse.mybir as mybir
from concourse.bass_utils import run_bass_kernel_spmd

F32 = mybir.dt.float32
BF16 = mybir.dt.bfloat16
AF = mybir.ActivationFunctionType
ALU = mybir.AluOpType
AX = mybir.AxisListType

D = 2048
T = 1024
NT = 16
NE = 32
NI = 6
NCORES = 8
EPS = 1e-5

SEM_KEYS = ("MM", "EVS", "EVV", "TP", "GV", "GS", "CC", "ST", "VC",
            "WC", "WX0", "WX1", "WX2", "WX3", "XR", "XS", "KVB", "KSR",
            "WR0", "WR1", "RWS", "RB0", "RB1")


class Ctx:

    def __init__(self, sems):
        self.sems = sems
        self.c = {k: 0 for k in SEM_KEYS}

    def inc(self, inst, key, n=1):
        inst.then_inc(self.sems[key], n)
        self.c[key] += n
        return self.c[key]

    def wait(self, eng, *specs):
        best = {}
        for spec in specs:
            if callable(spec):
                spec = spec()
            if not spec:
                continue
            key, val = spec
            if val and val > 0:
                best[key] = max(best.get(key, 0), val)
        for key, val in best.items():
            eng.wait_ge(self.sems[key], val)


class Pipeline:

    def __init__(self, cx, psums, barrier=()):
        self.cx = cx
        self.psums = psums
        self.barrier = list(barrier)
        self.entries = []
        self.fills = []

    def add(self, key, mm, evac_eng, evac, pe_waits=(), evac_waits=(),
            post=None, post_waits=(), tag=None, post_gates_psum=True):
        f = dict(kind="fill", key=key, mm=mm, eng=evac_eng, evac=evac,
                 pe_waits=list(pe_waits), evac_waits=list(evac_waits),
                 post=post, post_waits=list(post_waits), tag=tag,
                 post_gates_psum=post_gates_psum)
        self.entries.append(f)
        self.fills.append(f)
        return f

    def add_custom(self, pe=None, v=None, s=None, mm_incs=0):
        f = dict(kind="custom", pe=pe, v=v, s=s, mm_incs=mm_incs)
        self.entries.append(f)
        return f

    def plan(self):
        cx = self.cx
        evs, evv, mm = cx.c["EVS"], cx.c["EVV"], cx.c["MM"]
        last_on = {}
        for f in self.entries:
            if f["kind"] == "custom":
                mm += f["mm_incs"]
                f["mm_end"] = mm
                continue
            mm += 1
            f["mm_ms"] = mm
            if f["eng"] == "S":
                evs += 1
                f["evac_ms"] = ("EVS", evs)
            else:
                evv += 1
                f["evac_ms"] = ("EVV", evv)
            if f["post"] is not None:
                evv += 1
                f["post_ms"] = ("EVV", evv)
            f["prev_same"] = last_on.get(f["key"])
            last_on[f["key"]] = f
        self.planned = dict(MM=mm, EVS=evs, EVV=evv)
        return self

    def emit(self, blk):
        cx, psums = self.cx, self.psums

        def tbody(tensor):
            for f in self.entries:
                if f["kind"] == "custom":
                    if f["pe"]:
                        f["pe"](tensor)
                    continue
                waits = list(f["pe_waits"])
                p = f["prev_same"]
                if p is None:
                    waits += self.barrier
                else:
                    waits.append(p["evac_ms"])
                    if p["post"] is not None and p["post_gates_psum"]:
                        waits.append(p["post_ms"])
                cx.wait(tensor, *waits)
                inst = f["mm"](tensor, psums[f["key"]])
                got = cx.inc(inst, "MM")
                assert got == f["mm_ms"], (got, f["mm_ms"], f["tag"])

        def sbody(scalar):
            for f in self.entries:
                if f["kind"] == "custom":
                    if f["s"]:
                        f["s"](scalar)
                    continue
                if f["eng"] != "S":
                    continue
                cx.wait(scalar, ("MM", f["mm_ms"]), *f["evac_waits"])
                inst = f["evac"](scalar, psums[f["key"]])
                got = cx.inc(inst, "EVS")
                assert ("EVS", got) == f["evac_ms"], f["tag"]

        def vbody(vector):
            for f in self.entries:
                if f["kind"] == "custom":
                    if f["v"]:
                        f["v"](vector)
                    continue
                if f["eng"] == "V":
                    cx.wait(vector, ("MM", f["mm_ms"]), *f["evac_waits"])
                    inst = f["evac"](vector, psums[f["key"]])
                    got = cx.inc(inst, "EVV")
                    assert ("EVV", got) == f["evac_ms"], f["tag"]
                if f["post"] is not None:
                    cx.wait(vector, f["evac_ms"], *f["post_waits"])
                    inst = f["post"](vector)
                    got = cx.inc(inst, "EVV")
                    assert got == f["post_ms"][1], f["tag"]

        blk.tensor(tbody)
        blk.scalar(sbody)
        blk.vector(vbody)
        for k, v in self.planned.items():
            assert cx.c[k] == v, (k, cx.c[k], v)
        return [("MM", self.planned["MM"]), ("EVS", self.planned["EVS"]),
                ("EVV", self.planned["EVV"])]


def build_nc():
    nc = bass.Bass("TRN2", target_bir_lowering=False, debug=False)

    xT = nc.dram_tensor("xT", [NT, 128, T], F32, kind="ExternalInput").ap()
    wq = nc.dram_tensor("wq", [4, 128, 8192], F32, kind="ExternalInput").ap()
    wk = nc.dram_tensor("wk", [4, 128, 8192], F32, kind="ExternalInput").ap()
    wv = nc.dram_tensor("wv", [4, 128, 8192], F32, kind="ExternalInput").ap()
    wo = nc.dram_tensor("wo", [4, 128, 8192], F32, kind="ExternalInput").ap()
    boc = nc.dram_tensor("boc", [128, 16], F32, kind="ExternalInput").ap()
    gatep = nc.dram_tensor("gatep", [128, 512], F32, kind="ExternalInput").ap()
    gup = nc.dram_tensor("gup", [NE, 3, 128, 8192], F32,
                         kind="ExternalInput").ap()
    dwn = nc.dram_tensor("dwn", [NE, 2, 128, 6144], F32,
                         kind="ExternalInput").ap()
    cid = nc.dram_tensor("cid", [128, 128], F32, kind="ExternalInput").ap()
    cpat = nc.dram_tensor("cpat", [128, 2], F32, kind="ExternalInput").ap()
    cpatT = nc.dram_tensor("cpatT", [2, 128], F32, kind="ExternalInput").ap()
    conec = nc.dram_tensor("conec", [128, 1], F32, kind="ExternalInput").ap()
    coner = nc.dram_tensor("coner", [1, 128], F32, kind="ExternalInput").ap()
    out = nc.dram_tensor("out", [D, T], F32, kind="ExternalOutput").ap()
    cc_in = nc.dram_tensor("cc_in", [128, 1040], F32)
    cc_out = nc.dram_tensor("cc_out", [128, 1040], F32)
    rw_dram = nc.dram_tensor("rw_dram", [32, 1024], BF16).ap()

    with ExitStack() as ctx:
        E = ctx.enter_context
        AR_RES = E(nc.sbuf_tensor("AR_RES", [128, NT * T], F32))
        AR_H = E(nc.sbuf_tensor("AR_H", [128, NT * T], BF16))
        AR_Q = E(nc.sbuf_tensor("AR_Q", [128, NT * T], BF16))
        AR_W = E(nc.sbuf_tensor("AR_W", [128, 16384], BF16))
        SCR = E(nc.sbuf_tensor("SCR", [128, 2048], F32))
        SROW_A = E(nc.sbuf_tensor("SROW_A", [1, T], F32))
        SROW_B = E(nc.sbuf_tensor("SROW_B", [1, T], F32))
        n_sb = E(nc.sbuf_tensor("n_sb", [2, T], F32))
        r_f = E(nc.sbuf_tensor("r_f", [2, T], F32))
        r_bf = E(nc.sbuf_tensor("r_bf", [2, T], BF16))
        qk_t = E(nc.sbuf_tensor("qk_t", [128, T], BF16))
        kv_bf = E(nc.sbuf_tensor("kv_bf", [128, 1024], BF16))
        keysum = E(nc.sbuf_tensor("keysum", [128, 16], F32))
        probs_tm = E(nc.sbuf_tensor("probs_tm", [128, 256], F32))
        e_t = E(nc.sbuf_tensor("e_t", [128, 256], F32))
        rw1 = E(nc.sbuf_tensor("rw1", [128, 256], F32))
        rw_tm = E(nc.sbuf_tensor("rw_tm", [128, 256], F32))
        rw_em_bf = E(nc.sbuf_tensor("rw_em_bf", [32, 1024], BF16))
        top8 = E(nc.sbuf_tensor("top8", [128, 8], F32))
        scol = E(nc.sbuf_tensor("scol", [128, 8], F32))
        rcol = E(nc.sbuf_tensor("rcol", [128, 8], F32))
        s4col = E(nc.sbuf_tensor("s4col", [128, 8], F32))
        r4col = E(nc.sbuf_tensor("r4col", [128, 8], F32))
        rwb_bf = E(nc.sbuf_tensor("rwb_bf", [128, 2048], BF16))
        gate_bf = E(nc.sbuf_tensor("gate_bf", [128, 512], BF16))
        bo_sb = E(nc.sbuf_tensor("bo_sb", [128, 16], F32))
        idf = E(nc.sbuf_tensor("idf", [128, 128], F32))
        idb = E(nc.sbuf_tensor("idb", [128, 128], BF16))
        pat_bf = E(nc.sbuf_tensor("pat_bf", [128, 2], BF16))
        patT_bf = E(nc.sbuf_tensor("patT_bf", [2, 128], BF16))
        onec_bf = E(nc.sbuf_tensor("onec_bf", [128, 1], BF16))
        oner_f = E(nc.sbuf_tensor("oner_f", [1, 128], F32))
        PS_A = E(nc.psum_tensor("PS_A", [128, 1024], F32))
        PS_B = E(nc.psum_tensor("PS_B", [128, 1024], F32))
        PS_KV = E(nc.psum_tensor("PS_KV", [128, 1024], F32))
        PS_TP = E(nc.psum_tensor("PS_TP", [128, 1024], BF16))
        PS_SP = E(nc.psum_tensor("PS_SP", [128, 512], F32))
        sems = {k: E(nc.semaphore("s" + k)) for k in SEM_KEYS}
        blk = E(nc.Block())

        cx = Ctx(sems)
        psums = {"A": PS_A, "B": PS_B, "KV": PS_KV}
        arb = AR_RES.ap().bitcast(BF16)
        tpf = PS_TP.ap().bitcast(F32)

        def h_t(i):
            return AR_H[:, T * i:T * (i + 1)]

        def q_t(i):
            return AR_Q[:, T * i:T * (i + 1)]

        def res_t(i):
            return AR_RES[:, T * i:T * (i + 1)]

        def kq_t(dt):
            return arb[:, T * dt:T * (dt + 1)]

        def vq_t(dt):
            return arb[:, 4096 + T * dt:4096 + T * (dt + 1)]

        ktp = arb[:, 8192:9216]
        vtp = arb[:, 9216:10240]

        wplan = []
        wms = {}
        dcount = {k: 0 for k in SEM_KEYS}

        def plan_dma(name, key):
            dcount[key] += 16
            wms[name] = (key, dcount[key])
            return wms[name]

        def add_plain(name, dst, src, key):
            wplan.append((name, "plain", (dst, src), key))
            plan_dma(name, key)

        ring_entries = []

        def slot_ap(idx, width):
            base = 8192 * (idx % 2)
            return AR_W[:, base:base + width]

        def add_ring(name, src, width=8192):
            idx = len(ring_entries)
            ring_entries.append(name)
            key = "WR0" if idx % 2 == 0 else "WR1"
            wplan.append((name, "ring", (idx, width, src), key))
            plan_dma(name, key)
            return idx

        for nm, dst, src in [("idb", idb[:, :], cid),
                             ("pat_bf", pat_bf[:, :], cpat),
                             ("patT_bf", patT_bf[:, :], cpatT),
                             ("gate_bf", gate_bf[:, :], gatep),
                             ("onec_bf", onec_bf[:, :], conec),
                             ("idf", idf[:, :], cid),
                             ("bo_sb", bo_sb[:, :], boc),
                             ("oner_f", oner_f[:, :], coner)]:
            add_plain(nm, dst, src, "WC")
        for i in range(NT):
            add_plain(f"x{i}", h_t(i), xT[i, :, :], f"WX{i // 4}")
        WC_DONE = ("WC", dcount["WC"])
        WX_DONE = [(f"WX{g}", dcount[f"WX{g}"]) for g in range(4)]
        ring_idx = {}
        for q in range(4):
            ring_idx[f"wk{q}"] = add_ring(f"wk{q}", wk[q, :, :])
            ring_idx[f"wv{q}"] = add_ring(f"wv{q}", wv[q, :, :])
        for q in range(2):
            ring_idx[f"wq{q}"] = add_ring(f"wq{q}", wq[q, :, :])
        wplan.append(("__collective__", "cc", None, None))
        add_plain("kv_bf", kv_bf[:, :], cc_out[:, 0:1024], "KVB")
        add_plain("keysum_r", keysum[:, :], cc_out[:, 1024:1040], "KSR")
        for q in range(2, 4):
            ring_idx[f"wq{q}"] = add_ring(f"wq{q}", wq[q, :, :])
        for q in range(4):
            ring_idx[f"wo{q}"] = add_ring(f"wo{q}", wo[q, :, :])
        for e in range(NE):
            for t in range(3):
                ring_idx[f"g{e}_{t}"] = add_ring(f"g{e}_{t}", gup[e, t, :, :])
            for hf in range(2):
                ring_idx[f"d{e}_{hf}"] = add_ring(f"d{e}_{hf}",
                                                  dwn[e, hf, :, :], width=6144)
        ring_consumed = {}

        def proj_fill_mm(wname, dt, src_tile_fn):
            def mm(tensor, psum):
                for s in range(NT):
                    lo = 512 * s + 128 * dt
                    st, sp = (s == 0), (s == NT - 1)
                    for hh in range(2):
                        c = slice(512 * hh, 512 * (hh + 1))
                        inst = tensor.matmul(
                            psum[:, c],
                            slot_ap(ring_idx[wname], 8192)[:, lo:lo + 128],
                            src_tile_fn(s)[:, c], start=st, stop=sp)
                return inst
            return mm

        g_sq = []

        def v0a(vector):
            for i in range(NT):
                if i % 4 == 0:
                    cx.wait(vector, WX_DONE[i // 4])
                inst = vector.tensor_mul(q_t(i), h_t(i), h_t(i))
                g_sq.append(cx.inc(inst, "GV"))
        blk.vector(v0a)

        def ln_stats_and_h(sq_ready):
            ms = {}

            def t_stats(tensor):
                cx.wait(tensor, WC_DONE)
                for i in range(NT):
                    cx.wait(tensor, ("GV", sq_ready[i]))
                    st, sp = (i == 0), (i == NT - 1)
                    for hh in range(2):
                        c = slice(512 * hh, 512 * (hh + 1))
                        tensor.matmul(PS_A[0:1, c], onec_bf[:, :],
                                      h_t(i)[:, c], start=st, stop=sp)
                        inst = tensor.matmul(PS_B[0:1, c], onec_bf[:, :],
                                             q_t(i)[:, c], start=st, stop=sp)
                ms["stats"] = cx.inc(inst, "MM")
            blk.tensor(t_stats)

            def v_rows(vector):
                cx.wait(vector, ("MM", ms["stats"]))

                def step(inst):
                    c = cx.inc(inst, "VC")
                    vector.wait_ge(sems["VC"], c)
                step(vector.tensor_copy(SROW_A[:, :], PS_A[0:1, :]))
                step(vector.tensor_copy(SROW_B[:, :], PS_B[0:1, :]))
                step(vector.tensor_scalar_mul(SROW_A[:, :], SROW_A[:, :],
                                              1.0 / D))
                step(vector.tensor_scalar_mul(SROW_B[:, :], SROW_B[:, :],
                                              1.0 / D))
                step(vector.tensor_mul(n_sb[0:1, :], SROW_A[:, :],
                                       SROW_A[:, :]))
                step(vector.tensor_sub(SROW_B[:, :], SROW_B[:, :],
                                       n_sb[0:1, :]))
                inst = vector.tensor_scalar_add(SROW_B[:, :], SROW_B[:, :],
                                                EPS)
                ms["var"] = cx.inc(inst, "GV")
            blk.vector(v_rows)

            def s_rows(scalar):
                cx.wait(scalar, ("GV", ms["var"]))
                inst = scalar.activation(n_sb[0:1, :], SROW_B[:, :], AF.Ln)
                m1 = cx.inc(inst, "GS")
                cx.wait(scalar, ("GS", m1))
                inst = scalar.activation(SROW_B[:, :], n_sb[0:1, :], AF.Exp,
                                         scale=-0.5)
                ms["s"] = cx.inc(inst, "GS")
            blk.scalar(s_rows)

            def v_msrow(vector):
                cx.wait(vector, ("GS", ms["s"]))
                inst = vector.tensor_mul(SROW_A[:, :], SROW_A[:, :],
                                         SROW_B[:, :])
                ms["msrow"] = cx.inc(inst, "GV")
            blk.vector(v_msrow)

            def t_bcast(tensor):
                cx.wait(tensor, ("GV", ms["msrow"]), WC_DONE)
                for hh in range(2):
                    c = slice(512 * hh, 512 * (hh + 1))
                    tensor.matmul(PS_A[:, c], oner_f[:, :], SROW_B[0:1, c],
                                  start=True, stop=True)
                    inst = tensor.matmul(PS_B[:, c], oner_f[:, :],
                                         SROW_A[0:1, c], start=True, stop=True)
                ms["bc"] = cx.inc(inst, "MM")
            blk.tensor(t_bcast)

            h_done = []

            def v_apply(vector):
                cx.wait(vector, ("MM", ms["bc"]))
                for i in range(NT):
                    c = cx.inc(vector.tensor_mul(h_t(i), h_t(i), PS_A[:, :]),
                               "VC")
                    vector.wait_ge(sems["VC"], c)
                    inst = vector.tensor_sub(h_t(i), h_t(i), PS_B[:, :])
                    h_done.append(cx.inc(inst, "GV"))
                ms["bc_ev"] = h_done[-1]
            blk.vector(v_apply)
            return ms, h_done

        ms0, h_done = ln_stats_and_h(g_sq)

        ph1_barrier = [("GV", h_done[-1]), ("GV", ms0["bc_ev"])]
        pipe1 = Pipeline(cx, psums, barrier=ph1_barrier)
        tp0 = cx.c["TP"]
        kv_ms = {}
        kfills, vfills = [], []
        for q in range(4):
            for dt in range(4):
                gi = 4 * q + dt

                def k_evac(scalar, psum, dt=dt):
                    return scalar.activation(kq_t(dt), psum[:, :], AF.Relu)

                def k_post(vector, gi=gi, dt=dt):
                    return vector.reduce_sum(keysum[:, gi:gi + 1], kq_t(dt),
                                             axis=AX.X)
                kfills.append(pipe1.add(
                    "A" if gi % 2 == 0 else "B", proj_fill_mm(f"wk{q}", dt, h_t),
                    "S", k_evac, pe_waits=[wms[f"wk{q}"]],
                    post=k_post, tag=f"k{gi}"))
            for dt in range(4):
                gi = 4 * q + dt

                def v_evac(scalar, psum, dt=dt):
                    return scalar.activation(vq_t(dt), psum[:, :], AF.Identity)
                vfills.append(pipe1.add(
                    "A" if gi % 2 == 0 else "B", proj_fill_mm(f"wv{q}", dt, h_t),
                    "S", v_evac, pe_waits=[wms[f"wv{q}"]], tag=f"v{gi}"))

            def tkv_pe(tensor, q=q):
                if q == 0:
                    cx.wait(tensor, WC_DONE)
                for dt in range(4):
                    gi = 4 * q + dt
                    cx.wait(tensor, ("TP", tp0 + 2 * gi),
                            kfills[gi]["evac_ms"])
                    for b in range(8):
                        inst = tensor.transpose(
                            PS_TP[:, 128 * b:128 * (b + 1)],
                            kq_t(dt)[:, 128 * b:128 * (b + 1)], idb[:, :])
                    kv_ms[f"ktr{gi}"] = cx.inc(inst, "MM")
                    cx.wait(tensor, ("TP", tp0 + 2 * gi + 1),
                            vfills[gi]["evac_ms"])
                    for b in range(8):
                        inst = tensor.transpose(
                            PS_TP[:, 128 * b:128 * (b + 1)],
                            vq_t(dt)[:, 128 * b:128 * (b + 1)], idb[:, :])
                    kv_ms[f"vtr{gi}"] = cx.inc(inst, "MM")
                    cx.wait(tensor, ("TP", tp0 + 2 * gi + 2))
                    co = slice(64 * gi, 64 * gi + 64)
                    for b in range(8):
                        tensor.matmul(PS_KV[0:64, co],
                                      ktp[:, 128 * b:128 * b + 64],
                                      vtp[:, 128 * b:128 * b + 64],
                                      start=(b == 0), stop=(b == 7))
                    for b in range(8):
                        inst = tensor.matmul(
                            PS_KV[64:128, co],
                            ktp[:, 128 * b + 64:128 * b + 128],
                            vtp[:, 128 * b + 64:128 * b + 128],
                            start=(b == 0), stop=(b == 7))
                    kv_ms[f"kvmm{gi}"] = cx.inc(inst, "MM")

            def tkv_v(vector, q=q):
                for dt in range(4):
                    gi = 4 * q + dt
                    cx.wait(vector, ("MM", kv_ms[f"ktr{gi}"]))
                    inst = vector.tensor_copy(ktp, PS_TP[:, :])
                    got = cx.inc(inst, "TP")
                    assert got == tp0 + 2 * gi + 1
                    cx.wait(vector, ("MM", kv_ms[f"vtr{gi}"]))
                    inst = vector.tensor_copy(vtp, PS_TP[:, :])
                    got = cx.inc(inst, "TP")
                    assert got == tp0 + 2 * gi + 2
            pipe1.add_custom(pe=tkv_pe, v=tkv_v, mm_incs=12)
        pipe1.plan()
        for q in range(4):
            ring_consumed[f"wk{q}"] = kfills[4 * q + 3]["mm_ms"]
            ring_consumed[f"wv{q}"] = vfills[4 * q + 3]["mm_ms"]
        p1_end = pipe1.emit(blk)
        kv_done_planned = kv_ms["kvmm15"]

        ms_kvsb = {}

        def v_kvsb(vector):
            cx.wait(vector, ("MM", kv_ms["kvmm15"]))
            inst = vector.tensor_copy(SCR[:, 0:1024], PS_KV[:, :])
            ms_kvsb["v"] = cx.inc(inst, "GV")
        blk.vector(v_kvsb)

        def sync2(sync):
            cx.wait(sync, ("GV", ms_kvsb["v"]))
            inst = sync.dma_start(out=cc_in[:, 0:1024], in_=SCR[:, 0:1024])
            cx.inc(inst, "XS", 16)
            inst = sync.dma_start(out=cc_in[:, 1024:1040], in_=keysum[:, :])
            cx.inc(inst, "XS", 16)
            for i in range(NT):
                inst = sync.dma_start(out=res_t(i), in_=xT[i, :, :])
                cx.inc(inst, "XR", 16)
        blk.sync(sync2)
        XS_DONE = ("XS", cx.c["XS"])
        XR_DONE = ("XR", cx.c["XR"])

        pipe2 = Pipeline(cx, psums, barrier=p1_end)
        qfills = []
        for gi in range(NT):
            q, dt = gi // 4, gi % 4

            def q_evac(scalar, psum, gi=gi):
                return scalar.activation(q_t(gi), psum[:, :], AF.Relu)
            qfills.append(pipe2.add(
                "A" if gi % 2 == 0 else "B", proj_fill_mm(f"wq{q}", dt, h_t),
                "S", q_evac, pe_waits=[wms[f"wq{q}"]], tag=f"q{gi}"))
        pipe2.plan()
        for q in range(4):
            ring_consumed[f"wq{q}"] = qfills[4 * q + 3]["mm_ms"]
        p2_end = pipe2.emit(blk)

        evv30 = cx.c["EVV"]
        mm30 = cx.c["MM"]

        def vp(t, k):
            return ("EVV", evv30 + 8 * t + k)

        def pm(t, k):
            return ("MM", mm30 + 3 * t + k)

        def t3(tensor):
            cx.wait(tensor, WC_DONE, wms["kv_bf"], *p2_end)
            for t in range(NT):
                cx.wait(tensor, vp(t, 1))
                if t > 0:
                    cx.wait(tensor, vp(t - 1, 3))
                tensor.matmul(tpf[0:2, 0:512], pat_bf[:, :], qk_t[:, 0:512],
                              start=True, stop=True)
                inst = tensor.matmul(PS_SP[0:2, 0:512], pat_bf[:, :],
                                     qk_t[:, 512:1024], start=True, stop=True)
                cx.inc(inst, "MM")
                cx.wait(tensor, vp(t, 6))
                if t > 0:
                    cx.wait(tensor, vp(t - 1, 7))
                for hh in range(2):
                    c = slice(512 * hh, 512 * (hh + 1))
                    inst = tensor.matmul(PS_A[:, c], patT_bf[:, :],
                                         r_bf[:, c], start=True, stop=True)
                cx.inc(inst, "MM")
                if t > 1:
                    cx.wait(tensor, vp(t - 2, 8))
                ps = PS_KV if t % 2 == 0 else PS_B
                co = slice(64 * t, 64 * t + 64)
                for hh in range(2):
                    c = slice(512 * hh, 512 * (hh + 1))
                    tensor.matmul(ps[0:64, c], kv_bf[0:64, co],
                                  q_t(t)[0:64, c], start=True, stop=True)
                    inst = tensor.matmul(ps[64:128, c], kv_bf[64:128, co],
                                         q_t(t)[64:128, c], start=True,
                                         stop=True)
                cx.inc(inst, "MM")
        blk.tensor(t3)

        def v3(vector):
            cx.wait(vector, wms["keysum_r"])
            for t in range(NT):
                cx.wait(vector, qfills[t]["evac_ms"])
                if t > 0:
                    cx.wait(vector, pm(t - 1, 1))
                inst = vector.tensor_scalar_mul(qk_t[:, :], q_t(t),
                                                keysum[:, t:t + 1])
                cx.inc(inst, "EVV")
                cx.wait(vector, pm(t, 1))
                inst = vector.tensor_copy(n_sb[0:2, 0:512], tpf[0:2, 0:512])
                cx.inc(inst, "EVV")
                inst = vector.tensor_copy(n_sb[0:2, 512:1024],
                                          PS_SP[0:2, 0:512])
                cx.inc(inst, "EVV")
                cx.wait(vector, vp(t, 3))
                inst = vector.tensor_scalar_max(n_sb[0:2, :], n_sb[0:2, :],
                                                1e-6)
                cx.inc(inst, "EVV")
                cx.wait(vector, vp(t, 4))
                inst = vector.reciprocal(r_f[0:2, :], n_sb[0:2, :])
                cx.inc(inst, "EVV")
                cx.wait(vector, vp(t, 5))
                inst = vector.tensor_copy(r_bf[0:2, :], r_f[0:2, :])
                cx.inc(inst, "EVV")
                cx.wait(vector, pm(t, 2))
                rb_dst = SCR[:, 0:1024] if t % 2 == 0 else SCR[:, 1024:2048]
                inst = vector.tensor_copy(rb_dst, PS_A[:, :])
                cx.inc(inst, "EVV")
                cx.wait(vector, pm(t, 3), vp(t, 7))
                ps = PS_KV if t % 2 == 0 else PS_B
                inst = vector.tensor_mul(q_t(t), ps[:, :], rb_dst)
                cx.inc(inst, "EVV")
        blk.vector(v3)
        assert cx.c["EVV"] == evv30 + 8 * NT
        assert cx.c["MM"] == mm30 + 3 * NT
        p3_end = [("EVV", cx.c["EVV"]), ("MM", cx.c["MM"]),
                  ("EVS", cx.c["EVS"])]

        pipe4 = Pipeline(cx, psums, barrier=p3_end)
        wofills = []
        for gi in range(NT):
            q, dt = gi // 4, gi % 4

            def wo_evac(scalar, psum, gi=gi):
                tdst = SCR[:, 0:1024] if gi % 2 == 0 else SCR[:, 1024:2048]
                return scalar.activation(tdst, psum[:, :], AF.Identity,
                                         bias=bo_sb[:, gi:gi + 1])

            def wo_post(vector, gi=gi):
                tsrc = SCR[:, 0:1024] if gi % 2 == 0 else SCR[:, 1024:2048]
                return vector.tensor_add(res_t(gi), res_t(gi), tsrc)
            wofills.append(pipe4.add(
                "A" if gi % 2 == 0 else "B", proj_fill_mm(f"wo{q}", dt, q_t),
                "S", wo_evac,
                pe_waits=[wms[f"wo{q}"], WC_DONE],
                post=wo_post, post_waits=[XR_DONE],
                tag=f"wo{gi}"))
        pipe4.plan()
        for gi in range(2, NT):
            wofills[gi]["evac_waits"].append(wofills[gi - 2]["post_ms"])
        for q in range(4):
            ring_consumed[f"wo{q}"] = wofills[4 * q + 3]["mm_ms"]
        p4_end = pipe4.emit(blk)

        g_sq2 = []

        def v5a(vector):
            cx.wait(vector, *p4_end)
            for i in range(NT):
                c = cx.inc(vector.tensor_copy(h_t(i), res_t(i)), "VC")
                vector.wait_ge(sems["VC"], c)
                inst = vector.tensor_mul(q_t(i), h_t(i), h_t(i))
                g_sq2.append(cx.inc(inst, "GV"))
        blk.vector(v5a)
        ms5, h2_done = ln_stats_and_h(g_sq2)

        ms6 = {}

        def c6_logits_pe(tensor):
            for hh in range(2):
                c = slice(512 * hh, 512 * (hh + 1))
                for s in range(NT):
                    inst = tensor.matmul(PS_KV[0:32, c],
                                         gate_bf[:, 32 * s:32 * (s + 1)],
                                         h_t(s)[:, c],
                                         start=(s == 0), stop=(s == NT - 1))
            assert cx.inc(inst, "MM") == ms6["logits"]

        def c6_logits_v(vector):
            cx.wait(vector, ("MM", ms6["logits"]))
            inst = vector.tensor_copy(SCR[0:32, 0:1024], PS_KV[0:32, :])
            assert cx.inc(inst, "GV") == ms6["lg"]

        def c6_tr_pe(tensor):
            cx.wait(tensor, ("GV", ms6["lg"]), WC_DONE)
            for b in range(8):
                inst = tensor.transpose(
                    tpf[:, 32 * b:32 * (b + 1)],
                    SCR[0:32, 128 * b:128 * (b + 1)], idf[0:32, 0:32])
            assert cx.inc(inst, "MM") == ms6["tr"]

        def c6_tr_v(vector):
            cx.wait(vector, ("MM", ms6["tr"]))
            inst = vector.tensor_copy(probs_tm[:, 0:256], tpf[:, 0:256])
            assert cx.inc(inst, "GV") == ms6["trv"]

        def c6_soft_s(scalar):
            cx.wait(scalar, ("GV", ms6["trv"]))
            inst = scalar.activation(e_t[:, 0:256], probs_tm[:, 0:256],
                                     AF.Exp)
            assert cx.inc(inst, "GS") == ms6["exp"]

        def c6_soft_v(vector):
            def step(inst):
                c = cx.inc(inst, "VC")
                vector.wait_ge(sems["VC"], c)
            cx.wait(vector, ("GS", ms6["exp"]))
            for b in range(8):
                eb = e_t[:, 32 * b:32 * (b + 1)]
                step(vector.reduce_sum(scol[:, b:b + 1], eb, axis=AX.X))
                step(vector.reciprocal(rcol[:, b:b + 1], scol[:, b:b + 1]))
                pb = probs_tm[:, 32 * b:32 * (b + 1)]
                step(vector.tensor_scalar_mul(pb, eb, rcol[:, b:b + 1]))
                step(vector.max(out=top8[:, :], in_=pb))
                step(vector.reduce_sum(s4col[:, b:b + 1], top8[:, 0:4],
                                       axis=AX.X))
                step(vector.reciprocal(r4col[:, b:b + 1], s4col[:, b:b + 1]))
                rb1 = rw1[:, 32 * b:32 * (b + 1)]
                step(vector.tensor_scalar(rb1, pb, top8[:, 3:4], None,
                                          op0=ALU.is_ge))
                step(vector.tensor_mul(rb1, rb1, pb))
                inst = vector.tensor_scalar_mul(rw_tm[:, 32 * b:32 * (b + 1)],
                                                rb1, r4col[:, b:b + 1])
                assert cx.inc(inst, "GV") == ms6[f"rw{b}"]

        def c6_rt_pe(tensor):
            cx.wait(tensor, ("GV", ms6["rw7"]), ("GV", ms6["trv"]))
            for b in range(8):
                dst = (tpf[0:32, 128 * b:128 * (b + 1)] if b < 4 else
                       PS_SP[0:32, 128 * (b - 4):128 * (b - 3)])
                inst = tensor.transpose(dst, rw_tm[:, 32 * b:32 * (b + 1)],
                                        idf[:, :])
            assert cx.inc(inst, "MM") == ms6["rt"]

        def c6_rt_v(vector):
            def step(inst):
                c = cx.inc(inst, "VC")
                vector.wait_ge(sems["VC"], c)
            cx.wait(vector, ("MM", ms6["rt"]))
            step(vector.tensor_copy(SCR[0:32, 1024:1536], tpf[0:32, 0:512]))
            step(vector.tensor_copy(SCR[0:32, 1536:2048],
                                    PS_SP[0:32, 0:512]))
            inst = vector.tensor_copy(rw_em_bf[:, :], SCR[0:32, 1024:2048])
            assert cx.inc(inst, "GV") == ms6["rw_bf"]

        pipe7 = Pipeline(cx, psums,
                         barrier=[("GV", h2_done[-1]),
                                  ("GV", ms5["bc_ev"])])
        gfills = {}
        ufills = {}
        dfills = {}

        def gup_mm(e, t, dt):
            def mm(tensor, psum):
                wn = f"g{e}_{t}"
                for s in range(NT):
                    lo = 512 * s + 128 * dt
                    st, sp = (s == 0), (s == NT - 1)
                    for hh in range(2):
                        c = slice(512 * hh, 512 * (hh + 1))
                        inst = tensor.matmul(
                            psum[:, c],
                            slot_ap(ring_idx[wn], 8192)[:, lo:lo + 128],
                            h_t(s)[:, c], start=st, stop=sp)
                return inst
            return mm

        def dwn_mm(e, hf, dt):
            def mm(tensor, psum):
                wn = f"d{e}_{hf}"
                for s in range(NI):
                    lo = 1024 * s + 128 * dt
                    st, sp = (s == 0), (s == NI - 1)
                    for hh in range(2):
                        c = slice(512 * hh, 512 * (hh + 1))
                        inst = tensor.matmul(
                            psum[:, c],
                            slot_ap(ring_idx[wn], 6144)[:, lo:lo + 128],
                            q_t(6 + s)[:, c], start=st, stop=sp)
                return inst
            return mm

        nab = [0]

        def abkey():
            nab[0] += 1
            return "A" if nab[0] % 2 == 1 else "B"

        c6_entries = {}
        gpost_ms = {}

        def make_gpost(e):
            def gpost_v(vector):
                rbw = ("RB0" if e % 2 == 0 else "RB1", 16 * (e // 2 + 1))
                src = rwb_bf[:, 1024 * (e % 2):1024 * (e % 2 + 1)]
                for j in range(6):
                    cx.wait(vector, gfills[(e, j)]["evac_ms"], rbw)
                    inst = vector.tensor_mul(q_t(j), q_t(j), src)
                    assert cx.inc(inst, "GV") == gpost_ms[(e, j)]
            return gpost_v

        for e in range(NE):
            for t in range(3):
                for dt in range(4):
                    j = 4 * t + dt
                    if j < 6:
                        def g_evac(scalar, psum, j=j):
                            return scalar.activation(q_t(j), psum[:, :],
                                                     AF.Silu)
                        gfills[(e, j)] = pipe7.add(
                            abkey(), gup_mm(e, t, dt), "S", g_evac,
                            pe_waits=[wms[f"g{e}_{t}"]],
                            tag=f"g{e}_{j}")
                        if e == 0 and j == 0:
                            c6_entries["logits"] = pipe7.add_custom(
                                pe=c6_logits_pe, v=c6_logits_v, mm_incs=1)
                        elif e == 0 and j == 1:
                            c6_entries["tr"] = pipe7.add_custom(
                                pe=c6_tr_pe, v=c6_tr_v, mm_incs=1)
                        elif e == 0 and j == 2:
                            c6_entries["soft"] = pipe7.add_custom(
                                s=c6_soft_s, v=c6_soft_v)
                        elif e == 0 and j == 3:
                            c6_entries["rt"] = pipe7.add_custom(
                                pe=c6_rt_pe, v=c6_rt_v, mm_incs=1)
                        if j == 5:
                            pipe7.add_custom(v=make_gpost(e))
                    else:
                        ju = j - 6

                        def u_evac(vector, psum, ju=ju):
                            return vector.tensor_mul(q_t(6 + ju), psum[:, :],
                                                     q_t(ju))
                        ufills[(e, ju)] = pipe7.add(
                            abkey(), gup_mm(e, t, dt), "V", u_evac,
                            pe_waits=[wms[f"g{e}_{t}"]],
                            evac_waits=[], tag=f"u{e}_{ju}")
            for hf in range(2):
                for dt in range(8):
                    gi = 8 * hf + dt

                    def d_evac(vector, psum, gi=gi):
                        return vector.tensor_add(res_t(gi), res_t(gi),
                                                 psum[:, :])
                    dfills[(e, gi)] = pipe7.add(
                        abkey(), dwn_mm(e, hf, dt), "V", d_evac,
                        pe_waits=[wms[f"d{e}_{hf}"]], tag=f"d{e}_{gi}")
        pipe7.plan()
        ms6["logits"] = c6_entries["logits"]["mm_end"]
        ms6["tr"] = c6_entries["tr"]["mm_end"]
        ms6["rt"] = c6_entries["rt"]["mm_end"]
        _g0 = cx.c["GV"]
        ms6["lg"] = _g0 + 1
        ms6["trv"] = _g0 + 2
        for b in range(8):
            ms6[f"rw{b}"] = _g0 + 3 + b
        ms6["rw_bf"] = _g0 + 11
        for e in range(NE):
            for j in range(6):
                gpost_ms[(e, j)] = _g0 + 11 + 6 * e + j + 1
        ms6["exp"] = cx.c["GS"] + 1
        for e in range(NE):
            for ju in range(6):
                ufills[(e, ju)]["evac_waits"].append(
                    ("GV", gpost_ms[(e, ju)]))
            for gi in range(NT):
                dfills[(e, gi)]["pe_waits"].append(
                    ufills[(e, 5)]["evac_ms"])
            if e > 0:
                for j in range(6):
                    gfills[(e, j)]["evac_waits"].append(
                        ufills[(e - 1, j)]["evac_ms"])
            ring_consumed[f"g{e}_0"] = gfills[(e, 3)]["mm_ms"]
            ring_consumed[f"g{e}_1"] = ufills[(e, 1)]["mm_ms"]
            ring_consumed[f"g{e}_2"] = ufills[(e, 5)]["mm_ms"]
            ring_consumed[f"d{e}_0"] = dfills[(e, 7)]["mm_ms"]
            ring_consumed[f"d{e}_1"] = dfills[(e, 15)]["mm_ms"]
        pipe7.emit(blk)

        def sync3(sync):
            cx.wait(sync, ("GV", ms6["rw_bf"]))
            inst = sync.dma_start(out=rw_dram, in_=rw_em_bf[:, :])
            cx.inc(inst, "RWS", 16)
            sync.wait_ge(sems["RWS"], 16)
            for e in range(NE):
                key = "RB0" if e % 2 == 0 else "RB1"
                if e >= 2:
                    cx.wait(sync, ("GV", gpost_ms[(e - 2, 5)]),
                            (key, 16 * (e // 2)))
                dst = rwb_bf[:, 1024 * (e % 2):1024 * (e % 2 + 1)]
                inst = sync.dma_start(
                    out=dst,
                    in_=rw_dram[e:e + 1, :].broadcast_to([128, 1024]))
                cx.inc(inst, key, 16)
        blk.sync(sync3)

        def sync_out(sync):
            for gi in range(NT):
                cx.wait(sync, dfills[(NE - 1, gi)]["evac_ms"])
                inst = sync.dma_start(out=out[128 * gi:128 * (gi + 1), :],
                                      in_=res_t(gi))
                cx.inc(inst, "ST", 16)
            cx.wait(sync, ("ST", cx.c["ST"]))
        blk.sync(sync_out)

        def gp(gpsimd):
            for name, kind, info, key in wplan:
                if kind == "cc":
                    cx.wait(gpsimd, XS_DONE)
                    inst = gpsimd.collective_compute(
                        "AllReduce", ALU.add,
                        replica_groups=[list(range(NCORES))],
                        ins=[cc_in[:, :].opt()], outs=[cc_out[:, :].opt()])
                    cx.inc(inst, "CC", 1)
                    gpsimd.wait_ge(sems["CC"], 1)
                    continue
                if kind == "plain":
                    dst, src = info
                    inst = gpsimd.dma_start(out=dst, in_=src)
                else:
                    idx, width, src = info
                    if idx >= 2:
                        prev = ring_entries[idx - 2]
                        cx.wait(gpsimd, ("MM", ring_consumed[prev]),
                                (key, 16 * (idx // 2)))
                    inst = gpsimd.dma_start(out=slot_ap(idx, width), in_=src)
                got = cx.inc(inst, key, 16)
                assert (key, got) == wms[name], (name, key, got, wms[name])
        blk.gpsimd(gp)

    return nc


def _pack_weights(inputs):
    f = np.float32

    def attn(w):
        return np.ascontiguousarray(
            np.asarray(w, f).reshape(16, 128, 4, 512).transpose(2, 1, 0, 3)
        ).reshape(4, 128, 8192)

    d = {}
    d["wq"] = attn(inputs["L0_attn_Wq"])
    d["wk"] = attn(inputs["L0_attn_Wk"])
    d["wv"] = attn(inputs["L0_attn_Wv"])
    d["wo"] = attn(inputs["L0_attn_Wo"])
    d["boc"] = np.ascontiguousarray(
        np.asarray(inputs["L0_attn_bo"], f).reshape(16, 128).T)
    d["gatep"] = np.ascontiguousarray(
        np.asarray(inputs["L0_ffn_gate"], f).reshape(16, 128, 32)
        .transpose(1, 0, 2)).reshape(128, 512)
    g = np.asarray(inputs["L0_ffn_gate_up_proj"], f)
    d["gup"] = np.ascontiguousarray(
        g.reshape(32, 16, 128, 3, 512).transpose(0, 3, 2, 1, 4)
    ).reshape(32, 3, 128, 8192)
    dn = np.asarray(inputs["L0_ffn_down_proj"], f)
    d["dwn"] = np.ascontiguousarray(
        dn.reshape(32, 6, 128, 2, 1024).transpose(0, 3, 2, 1, 4)
    ).reshape(32, 2, 128, 6144)
    d["cid"] = np.eye(128, dtype=f)
    pat = np.zeros((128, 2), f)
    pat[:64, 0] = 1.0
    pat[64:, 1] = 1.0
    d["cpat"] = pat
    d["cpatT"] = np.ascontiguousarray(pat.T)
    d["conec"] = np.ones((128, 1), f)
    d["coner"] = np.ones((1, 128), f)
    return d


def kernel(L0_attn_hidden_states, L0_attn_Wq, L0_attn_Wk, L0_attn_Wv,
           L0_attn_Wo, L0_attn_bo, L0_ffn_gate, L0_ffn_gate_up_proj,
           L0_ffn_down_proj):
    shared = _pack_weights(dict(
        L0_attn_Wq=L0_attn_Wq, L0_attn_Wk=L0_attn_Wk, L0_attn_Wv=L0_attn_Wv,
        L0_attn_Wo=L0_attn_Wo, L0_attn_bo=L0_attn_bo, L0_ffn_gate=L0_ffn_gate,
        L0_ffn_gate_up_proj=L0_ffn_gate_up_proj,
        L0_ffn_down_proj=L0_ffn_down_proj))
    x = np.asarray(L0_attn_hidden_states, np.float32).reshape(8192, 2048)
    in_maps = []
    for c in range(NCORES):
        xc = np.ascontiguousarray(x[1024 * c:1024 * (c + 1), :].T)
        m = dict(shared)
        m["xT"] = xc.reshape(16, 128, 1024)
        in_maps.append(m)
    nc = build_nc()
    res = run_bass_kernel_spmd(nc, in_maps, core_ids=list(range(NCORES)))
    outp = np.empty((8192, 2048), np.float32)
    for c in range(NCORES):
        outp[1024 * c:1024 * (c + 1), :] = res.results[c]["out"].T
    return outp.reshape(1, 8192, 2048)


# version 05
# speedup vs baseline: 1.0222x; 1.0143x over previous
from contextlib import ExitStack

import numpy as np
import concourse.bass as bass
import concourse.mybir as mybir
from concourse.bass_utils import run_bass_kernel_spmd

F32 = mybir.dt.float32
BF16 = mybir.dt.bfloat16
AF = mybir.ActivationFunctionType
ALU = mybir.AluOpType
AX = mybir.AxisListType

D = 2048
T = 1024
NT = 16
NE = 32
NI = 6
NCORES = 8
EPS = 1e-5

SEM_KEYS = ("MM", "EVS", "EVV", "TP", "GV", "GS", "CC", "ST", "VC",
            "WC", "WX0", "WX1", "WX2", "WX3", "XR", "XS", "KVB", "KSR",
            "WR0", "WR1", "RWS", "RB0", "RB1")


class Ctx:

    def __init__(self, sems):
        self.sems = sems
        self.c = {k: 0 for k in SEM_KEYS}

    def inc(self, inst, key, n=1):
        inst.then_inc(self.sems[key], n)
        self.c[key] += n
        return self.c[key]

    def wait(self, eng, *specs):
        best = {}
        for spec in specs:
            if callable(spec):
                spec = spec()
            if not spec:
                continue
            key, val = spec
            if val and val > 0:
                best[key] = max(best.get(key, 0), val)
        for key, val in best.items():
            eng.wait_ge(self.sems[key], val)


class Pipeline:

    def __init__(self, cx, psums, barrier=()):
        self.cx = cx
        self.psums = psums
        self.barrier = list(barrier)
        self.entries = []
        self.fills = []

    def add(self, key, mm, evac_eng, evac, pe_waits=(), evac_waits=(),
            post=None, post_waits=(), tag=None, post_gates_psum=True):
        f = dict(kind="fill", key=key, mm=mm, eng=evac_eng, evac=evac,
                 pe_waits=list(pe_waits), evac_waits=list(evac_waits),
                 post=post, post_waits=list(post_waits), tag=tag,
                 post_gates_psum=post_gates_psum)
        self.entries.append(f)
        self.fills.append(f)
        return f

    def add_custom(self, pe=None, v=None, s=None, mm_incs=0):
        f = dict(kind="custom", pe=pe, v=v, s=s, mm_incs=mm_incs)
        self.entries.append(f)
        return f

    def plan(self):
        cx = self.cx
        evs, evv, mm = cx.c["EVS"], cx.c["EVV"], cx.c["MM"]
        last_on = {}
        for f in self.entries:
            if f["kind"] == "custom":
                mm += f["mm_incs"]
                f["mm_end"] = mm
                continue
            mm += 1
            f["mm_ms"] = mm
            if f["eng"] == "S":
                evs += 1
                f["evac_ms"] = ("EVS", evs)
            else:
                evv += 1
                f["evac_ms"] = ("EVV", evv)
            if f["post"] is not None:
                evv += 1
                f["post_ms"] = ("EVV", evv)
            f["prev_same"] = last_on.get(f["key"])
            last_on[f["key"]] = f
        self.planned = dict(MM=mm, EVS=evs, EVV=evv)
        return self

    def emit(self, blk):
        cx, psums = self.cx, self.psums

        def tbody(tensor):
            for f in self.entries:
                if f["kind"] == "custom":
                    if f["pe"]:
                        f["pe"](tensor)
                    continue
                waits = list(f["pe_waits"])
                p = f["prev_same"]
                if p is None:
                    waits += self.barrier
                else:
                    waits.append(p["evac_ms"])
                    if p["post"] is not None and p["post_gates_psum"]:
                        waits.append(p["post_ms"])
                cx.wait(tensor, *waits)
                inst = f["mm"](tensor, psums[f["key"]])
                got = cx.inc(inst, "MM")
                assert got == f["mm_ms"], (got, f["mm_ms"], f["tag"])

        def sbody(scalar):
            for f in self.entries:
                if f["kind"] == "custom":
                    if f["s"]:
                        f["s"](scalar)
                    continue
                if f["eng"] != "S":
                    continue
                cx.wait(scalar, ("MM", f["mm_ms"]), *f["evac_waits"])
                inst = f["evac"](scalar, psums[f["key"]])
                got = cx.inc(inst, "EVS")
                assert ("EVS", got) == f["evac_ms"], f["tag"]

        def vbody(vector):
            for f in self.entries:
                if f["kind"] == "custom":
                    if f["v"]:
                        f["v"](vector)
                    continue
                if f["eng"] == "V":
                    cx.wait(vector, ("MM", f["mm_ms"]), *f["evac_waits"])
                    inst = f["evac"](vector, psums[f["key"]])
                    got = cx.inc(inst, "EVV")
                    assert ("EVV", got) == f["evac_ms"], f["tag"]
                if f["post"] is not None:
                    cx.wait(vector, f["evac_ms"], *f["post_waits"])
                    inst = f["post"](vector)
                    got = cx.inc(inst, "EVV")
                    assert got == f["post_ms"][1], f["tag"]

        blk.tensor(tbody)
        blk.scalar(sbody)
        blk.vector(vbody)
        for k, v in self.planned.items():
            assert cx.c[k] == v, (k, cx.c[k], v)
        return [("MM", self.planned["MM"]), ("EVS", self.planned["EVS"]),
                ("EVV", self.planned["EVV"])]


def build_nc():
    nc = bass.Bass("TRN2", target_bir_lowering=False, debug=False)

    xT = nc.dram_tensor("xT", [NT, 128, T], F32, kind="ExternalInput").ap()
    wq = nc.dram_tensor("wq", [4, 128, 8192], F32, kind="ExternalInput").ap()
    wk = nc.dram_tensor("wk", [4, 128, 8192], F32, kind="ExternalInput").ap()
    wv = nc.dram_tensor("wv", [4, 128, 8192], F32, kind="ExternalInput").ap()
    wo = nc.dram_tensor("wo", [4, 128, 8192], F32, kind="ExternalInput").ap()
    boc = nc.dram_tensor("boc", [128, 16], F32, kind="ExternalInput").ap()
    gatep = nc.dram_tensor("gatep", [128, 512], F32, kind="ExternalInput").ap()
    gup = nc.dram_tensor("gup", [NE, 3, 128, 8192], F32,
                         kind="ExternalInput").ap()
    dwn = nc.dram_tensor("dwn", [NE, 2, 128, 6144], F32,
                         kind="ExternalInput").ap()
    cid = nc.dram_tensor("cid", [128, 128], F32, kind="ExternalInput").ap()
    cpat = nc.dram_tensor("cpat", [128, 2], F32, kind="ExternalInput").ap()
    cpatT = nc.dram_tensor("cpatT", [2, 128], F32, kind="ExternalInput").ap()
    conec = nc.dram_tensor("conec", [128, 1], F32, kind="ExternalInput").ap()
    coner = nc.dram_tensor("coner", [1, 128], F32, kind="ExternalInput").ap()
    out = nc.dram_tensor("out", [D, T], F32, kind="ExternalOutput").ap()
    cc_in = nc.dram_tensor("cc_in", [128, 1040], F32)
    cc_out = nc.dram_tensor("cc_out", [128, 1040], F32)
    rw_dram = nc.dram_tensor("rw_dram", [32, 1024], BF16).ap()

    with ExitStack() as ctx:
        E = ctx.enter_context
        AR_RES = E(nc.sbuf_tensor("AR_RES", [128, NT * T], F32))
        AR_H = E(nc.sbuf_tensor("AR_H", [128, NT * T], BF16))
        AR_Q = E(nc.sbuf_tensor("AR_Q", [128, NT * T], BF16))
        AR_W = E(nc.sbuf_tensor("AR_W", [128, 16384], BF16))
        SCR = E(nc.sbuf_tensor("SCR", [128, 2048], F32))
        SROW_A = E(nc.sbuf_tensor("SROW_A", [1, T], F32))
        SROW_B = E(nc.sbuf_tensor("SROW_B", [1, T], F32))
        n_sb = E(nc.sbuf_tensor("n_sb", [34, T], F32))
        r_f = E(nc.sbuf_tensor("r_f", [34, T], F32))
        r_bf = E(nc.sbuf_tensor("r_bf", [2, 2 * T], BF16))
        qk_t = E(nc.sbuf_tensor("qk_t", [128, 2 * T], BF16))
        kv_bf = E(nc.sbuf_tensor("kv_bf", [128, 1024], BF16))
        keysum = E(nc.sbuf_tensor("keysum", [128, 16], F32))
        probs_tm = E(nc.sbuf_tensor("probs_tm", [128, 256], F32))
        rw_tm = E(nc.sbuf_tensor("rw_tm", [128, 256], F32))
        rw_em_bf = E(nc.sbuf_tensor("rw_em_bf", [32, 1024], BF16))
        top8 = E(nc.sbuf_tensor("top8", [128, 64], F32))
        scol = E(nc.sbuf_tensor("scol", [128, 8], F32))
        rcol = E(nc.sbuf_tensor("rcol", [128, 8], F32))
        s4col = E(nc.sbuf_tensor("s4col", [128, 8], F32))
        r4col = E(nc.sbuf_tensor("r4col", [128, 8], F32))
        rwb_bf = qk_t
        gate_bf = E(nc.sbuf_tensor("gate_bf", [128, 512], BF16))
        bo_sb = E(nc.sbuf_tensor("bo_sb", [128, 16], F32))
        idf = E(nc.sbuf_tensor("idf", [128, 128], F32))
        idb = E(nc.sbuf_tensor("idb", [128, 128], BF16))
        pat_bf = E(nc.sbuf_tensor("pat_bf", [128, 2], BF16))
        patT_bf = E(nc.sbuf_tensor("patT_bf", [2, 128], BF16))
        onec_bf = E(nc.sbuf_tensor("onec_bf", [128, 1], BF16))
        oner_f = E(nc.sbuf_tensor("oner_f", [1, 128], F32))
        PS_A = E(nc.psum_tensor("PS_A", [128, 1024], F32))
        PS_B = E(nc.psum_tensor("PS_B", [128, 1024], F32))
        PS_KV = E(nc.psum_tensor("PS_KV", [128, 1024], F32))
        PS_TP = E(nc.psum_tensor("PS_TP", [128, 1024], BF16))
        PS_SP = E(nc.psum_tensor("PS_SP", [128, 512], F32))
        sems = {k: E(nc.semaphore("s" + k)) for k in SEM_KEYS}
        blk = E(nc.Block())

        cx = Ctx(sems)
        psums = {"A": PS_A, "B": PS_B, "KV": PS_KV}
        arb = AR_RES.ap().bitcast(BF16)
        tpf = PS_TP.ap().bitcast(F32)

        def h_t(i):
            return AR_H[:, T * i:T * (i + 1)]

        def q_t(i):
            return AR_Q[:, T * i:T * (i + 1)]

        def res_t(i):
            return AR_RES[:, T * i:T * (i + 1)]

        def kq_t(dt):
            return arb[:, T * dt:T * (dt + 1)]

        def vq_t(dt):
            return arb[:, 4096 + T * dt:4096 + T * (dt + 1)]

        ktp = arb[:, 8192:9216]
        vtp = arb[:, 9216:10240]

        wplan = []
        wms = {}
        dcount = {k: 0 for k in SEM_KEYS}

        def plan_dma(name, key):
            dcount[key] += 16
            wms[name] = (key, dcount[key])
            return wms[name]

        def add_plain(name, dst, src, key):
            wplan.append((name, "plain", (dst, src), key))
            plan_dma(name, key)

        ring_entries = []

        def slot_ap(idx, width):
            base = 8192 * (idx % 2)
            return AR_W[:, base:base + width]

        def add_ring(name, src, width=8192):
            idx = len(ring_entries)
            ring_entries.append(name)
            key = "WR0" if idx % 2 == 0 else "WR1"
            wplan.append((name, "ring", (idx, width, src), key))
            plan_dma(name, key)
            return idx

        for nm, dst, src in [("idb", idb[:, :], cid),
                             ("pat_bf", pat_bf[:, :], cpat),
                             ("patT_bf", patT_bf[:, :], cpatT),
                             ("gate_bf", gate_bf[:, :], gatep),
                             ("onec_bf", onec_bf[:, :], conec),
                             ("idf", idf[:, :], cid),
                             ("bo_sb", bo_sb[:, :], boc),
                             ("oner_f", oner_f[:, :], coner)]:
            add_plain(nm, dst, src, "WC")
        for i in range(NT):
            add_plain(f"x{i}", h_t(i), xT[i, :, :], f"WX{i // 4}")
        WC_DONE = ("WC", dcount["WC"])
        WX_DONE = [(f"WX{g}", dcount[f"WX{g}"]) for g in range(4)]
        ring_idx = {}
        for q in range(4):
            ring_idx[f"wk{q}"] = add_ring(f"wk{q}", wk[q, :, :])
            ring_idx[f"wv{q}"] = add_ring(f"wv{q}", wv[q, :, :])
        for q in range(2):
            ring_idx[f"wq{q}"] = add_ring(f"wq{q}", wq[q, :, :])
        wplan.append(("__collective__", "cc", None, None))
        for q in range(2, 4):
            ring_idx[f"wq{q}"] = add_ring(f"wq{q}", wq[q, :, :])
        for q in range(4):
            ring_idx[f"wo{q}"] = add_ring(f"wo{q}", wo[q, :, :])
        for e in range(NE):
            for t in range(3):
                ring_idx[f"g{e}_{t}"] = add_ring(f"g{e}_{t}", gup[e, t, :, :])
            for hf in range(2):
                ring_idx[f"d{e}_{hf}"] = add_ring(f"d{e}_{hf}",
                                                  dwn[e, hf, :, :], width=6144)
        ring_consumed = {}

        def proj_fill_mm(wname, dt, src_tile_fn):
            def mm(tensor, psum):
                for s in range(NT):
                    lo = 512 * s + 128 * dt
                    st, sp = (s == 0), (s == NT - 1)
                    for hh in range(2):
                        c = slice(512 * hh, 512 * (hh + 1))
                        inst = tensor.matmul(
                            psum[:, c],
                            slot_ap(ring_idx[wname], 8192)[:, lo:lo + 128],
                            src_tile_fn(s)[:, c], start=st, stop=sp)
                return inst
            return mm

        g_sq = []

        def v0a(vector):
            for i in range(NT):
                if i % 4 == 0:
                    cx.wait(vector, WX_DONE[i // 4])
                inst = vector.tensor_mul(q_t(i), h_t(i), h_t(i))
                g_sq.append(cx.inc(inst, "GV"))
        blk.vector(v0a)

        def ln_stats_and_h(sq_ready):
            ms = {}

            def t_stats(tensor):
                cx.wait(tensor, WC_DONE)
                for i in range(NT):
                    cx.wait(tensor, ("GV", sq_ready[i]))
                    st, sp = (i == 0), (i == NT - 1)
                    for hh in range(2):
                        c = slice(512 * hh, 512 * (hh + 1))
                        tensor.matmul(PS_A[0:1, c], onec_bf[:, :],
                                      h_t(i)[:, c], start=st, stop=sp)
                        inst = tensor.matmul(PS_B[0:1, c], onec_bf[:, :],
                                             q_t(i)[:, c], start=st, stop=sp)
                ms["stats"] = cx.inc(inst, "MM")
            blk.tensor(t_stats)

            def v_rows(vector):
                cx.wait(vector, ("MM", ms["stats"]))

                def step(inst):
                    c = cx.inc(inst, "VC")
                    vector.wait_ge(sems["VC"], c)
                step(vector.tensor_copy(SROW_A[:, :], PS_A[0:1, :]))
                step(vector.tensor_copy(SROW_B[:, :], PS_B[0:1, :]))
                step(vector.tensor_scalar_mul(SROW_A[:, :], SROW_A[:, :],
                                              1.0 / D))
                step(vector.tensor_scalar_mul(SROW_B[:, :], SROW_B[:, :],
                                              1.0 / D))
                step(vector.tensor_mul(n_sb[0:1, 0:T], SROW_A[:, :],
                                       SROW_A[:, :]))
                step(vector.tensor_sub(SROW_B[:, :], SROW_B[:, :],
                                       n_sb[0:1, 0:T]))
                inst = vector.tensor_scalar_add(SROW_B[:, :], SROW_B[:, :],
                                                EPS)
                ms["var"] = cx.inc(inst, "GV")
            blk.vector(v_rows)

            def s_rows(scalar):
                cx.wait(scalar, ("GV", ms["var"]))
                inst = scalar.activation(n_sb[0:1, 0:T], SROW_B[:, :], AF.Ln)
                m1 = cx.inc(inst, "GS")
                cx.wait(scalar, ("GS", m1))
                inst = scalar.activation(SROW_B[:, :], n_sb[0:1, 0:T],
                                         AF.Exp, scale=-0.5)
                ms["s"] = cx.inc(inst, "GS")
            blk.scalar(s_rows)

            def v_msrow(vector):
                cx.wait(vector, ("GS", ms["s"]))
                inst = vector.tensor_mul(SROW_A[:, :], SROW_A[:, :],
                                         SROW_B[:, :])
                ms["msrow"] = cx.inc(inst, "GV")
            blk.vector(v_msrow)

            def t_bcast(tensor):
                cx.wait(tensor, ("GV", ms["msrow"]), WC_DONE)
                for hh in range(2):
                    c = slice(512 * hh, 512 * (hh + 1))
                    tensor.matmul(PS_A[:, c], oner_f[:, :], SROW_B[0:1, c],
                                  start=True, stop=True)
                    inst = tensor.matmul(PS_B[:, c], oner_f[:, :],
                                         SROW_A[0:1, c], start=True, stop=True)
                ms["bc"] = cx.inc(inst, "MM")
            blk.tensor(t_bcast)

            h_done = []

            def v_apply(vector):
                cx.wait(vector, ("MM", ms["bc"]))
                for i in range(NT):
                    c = cx.inc(vector.tensor_mul(h_t(i), h_t(i), PS_A[:, :]),
                               "VC")
                    vector.wait_ge(sems["VC"], c)
                    inst = vector.tensor_sub(h_t(i), h_t(i), PS_B[:, :])
                    h_done.append(cx.inc(inst, "GV"))
                ms["bc_ev"] = h_done[-1]
            blk.vector(v_apply)
            return ms, h_done

        ms0, h_done = ln_stats_and_h(g_sq)

        ph1_barrier = [("GV", h_done[-1]), ("GV", ms0["bc_ev"])]
        pipe1 = Pipeline(cx, psums, barrier=ph1_barrier)
        tp0 = cx.c["TP"]
        kv_ms = {}
        kfills, vfills = [], []
        for q in range(4):
            for dt in range(4):
                gi = 4 * q + dt

                def k_evac(scalar, psum, dt=dt):
                    return scalar.activation(kq_t(dt), psum[:, :], AF.Relu)

                def k_post(vector, gi=gi, dt=dt):
                    return vector.reduce_sum(keysum[:, gi:gi + 1], kq_t(dt),
                                             axis=AX.X)
                kfills.append(pipe1.add(
                    "A" if gi % 2 == 0 else "B", proj_fill_mm(f"wk{q}", dt, h_t),
                    "S", k_evac, pe_waits=[wms[f"wk{q}"]],
                    post=k_post, tag=f"k{gi}"))
            for dt in range(4):
                gi = 4 * q + dt

                def v_evac(scalar, psum, dt=dt):
                    return scalar.activation(vq_t(dt), psum[:, :], AF.Identity)
                vfills.append(pipe1.add(
                    "A" if gi % 2 == 0 else "B", proj_fill_mm(f"wv{q}", dt, h_t),
                    "S", v_evac, pe_waits=[wms[f"wv{q}"]], tag=f"v{gi}"))

            def tkv_pe(tensor, q=q):
                if q == 0:
                    cx.wait(tensor, WC_DONE)
                for dt in range(4):
                    gi = 4 * q + dt
                    cx.wait(tensor, ("TP", tp0 + 2 * gi),
                            kfills[gi]["evac_ms"])
                    for b in range(8):
                        inst = tensor.transpose(
                            PS_TP[:, 128 * b:128 * (b + 1)],
                            kq_t(dt)[:, 128 * b:128 * (b + 1)], idb[:, :])
                    kv_ms[f"ktr{gi}"] = cx.inc(inst, "MM")
                    cx.wait(tensor, ("TP", tp0 + 2 * gi + 1),
                            vfills[gi]["evac_ms"])
                    for b in range(8):
                        inst = tensor.transpose(
                            PS_TP[:, 128 * b:128 * (b + 1)],
                            vq_t(dt)[:, 128 * b:128 * (b + 1)], idb[:, :])
                    kv_ms[f"vtr{gi}"] = cx.inc(inst, "MM")
                    cx.wait(tensor, ("TP", tp0 + 2 * gi + 2))
                    co = slice(64 * gi, 64 * gi + 64)
                    for b in range(8):
                        tensor.matmul(PS_KV[0:64, co],
                                      ktp[:, 128 * b:128 * b + 64],
                                      vtp[:, 128 * b:128 * b + 64],
                                      start=(b == 0), stop=(b == 7))
                    for b in range(8):
                        inst = tensor.matmul(
                            PS_KV[64:128, co],
                            ktp[:, 128 * b + 64:128 * b + 128],
                            vtp[:, 128 * b + 64:128 * b + 128],
                            start=(b == 0), stop=(b == 7))
                    kv_ms[f"kvmm{gi}"] = cx.inc(inst, "MM")

            def tkv_v(vector, q=q):
                for dt in range(4):
                    gi = 4 * q + dt
                    cx.wait(vector, ("MM", kv_ms[f"ktr{gi}"]))
                    inst = vector.tensor_copy(ktp, PS_TP[:, :])
                    got = cx.inc(inst, "TP")
                    assert got == tp0 + 2 * gi + 1
                    cx.wait(vector, ("MM", kv_ms[f"vtr{gi}"]))
                    inst = vector.tensor_copy(vtp, PS_TP[:, :])
                    got = cx.inc(inst, "TP")
                    assert got == tp0 + 2 * gi + 2
            pipe1.add_custom(pe=tkv_pe, v=tkv_v, mm_incs=12)
        pipe1.plan()
        for q in range(4):
            ring_consumed[f"wk{q}"] = kfills[4 * q + 3]["mm_ms"]
            ring_consumed[f"wv{q}"] = vfills[4 * q + 3]["mm_ms"]
        p1_end = pipe1.emit(blk)
        kv_done_planned = kv_ms["kvmm15"]

        ms_kvsb = {}

        def v_kvsb(vector):
            cx.wait(vector, ("MM", kv_ms["kvmm15"]))
            inst = vector.tensor_copy(SCR[:, 0:1024], PS_KV[:, :])
            ms_kvsb["v"] = cx.inc(inst, "GV")
        blk.vector(v_kvsb)

        def sync2(sync):
            cx.wait(sync, ("GV", ms_kvsb["v"]))
            inst = sync.dma_start(out=cc_in[:, 0:1024], in_=SCR[:, 0:1024])
            cx.inc(inst, "XS", 16)
            inst = sync.dma_start(out=cc_in[:, 1024:1040], in_=keysum[:, :])
            cx.inc(inst, "XS", 16)
            for i in range(NT):
                inst = sync.dma_start(out=res_t(i), in_=xT[i, :, :])
                cx.inc(inst, "XR", 16)
            sync.wait_ge(sems["CC"], 1)
            inst = sync.dma_start(out=SCR[:, 0:1024], in_=cc_out[:, 0:1024])
            cx.inc(inst, "KVB", 16)
            inst = sync.dma_start(out=keysum[:, :], in_=cc_out[:, 1024:1040])
            cx.inc(inst, "KSR", 16)
        blk.sync(sync2)
        XS_DONE = ("XS", cx.c["XS"])
        XR_DONE = ("XR", cx.c["XR"])

        pipe2 = Pipeline(cx, psums, barrier=p1_end)
        qfills = []
        for gi in range(NT):
            q, dt = gi // 4, gi % 4

            def q_evac(scalar, psum, gi=gi):
                return scalar.activation(q_t(gi), psum[:, :], AF.Relu)
            qfills.append(pipe2.add(
                "A" if gi % 2 == 0 else "B", proj_fill_mm(f"wq{q}", dt, h_t),
                "S", q_evac, pe_waits=[wms[f"wq{q}"]], tag=f"q{gi}"))
        pipe2.plan()
        for q in range(4):
            ring_consumed[f"wq{q}"] = qfills[4 * q + 3]["mm_ms"]
        p2_end = pipe2.emit(blk)

        mm0, evv0, gs0 = cx.c["MM"], cx.c["EVV"], cx.c["GS"]
        pe_order = [("PAT", 0), ("PAT", 1)]
        v_order = [("NRM", 0), ("NRM", 1)]
        for t in range(NT):
            pe_order += [("BC", t), ("QKV", t)]
            v_order += [("RBE", t), ("QKE", t)]
            if t + 2 < NT:
                pe_order.append(("PAT", t + 2))
                v_order.append(("NRM", t + 2))
        ms3mm = {}
        _c = mm0
        for op in pe_order:
            _c += 1
            ms3mm[op] = _c
        ms3v = {}
        _c = evv0
        for op in v_order:
            _c += {"NRM": 5, "RBE": 1, "QKE": 1}[op[0]]
            ms3v[op] = _c
        pm3 = {t: gs0 + t + 1 for t in range(NT)}
        kv_cast_ms = cx.c["GV"] + 1

        def qk_slot(t):
            return qk_t[:, 1024 * (t % 2):1024 * (t % 2) + 1024]

        def nsb_slot(t):
            b = 32 * (t % 2)
            return n_sb[b:b + 2, :]

        def rf_slot(t):
            b = 32 * (t % 2)
            return r_f[b:b + 2, :]

        def rbf_slot(t):
            return r_bf[0:2, 1024 * (t % 2):1024 * (t % 2) + 1024]

        def rb_slot(t):
            return SCR[:, 0:1024] if t % 2 == 0 else SCR[:, 1024:2048]

        def ps3(t):
            return PS_KV if t % 2 == 0 else PS_B

        def s3(scalar):
            cx.wait(scalar, ("KSR", 16))
            for t in range(NT):
                w = [qfills[t]["evac_ms"]]
                if t >= 2:
                    w.append(("MM", ms3mm[("PAT", t - 2)]))
                cx.wait(scalar, *w)
                inst = scalar.activation(qk_slot(t), q_t(t), AF.Identity,
                                         scale=keysum[:, t:t + 1])
                assert cx.inc(inst, "GS") == pm3[t]
        blk.scalar(s3)

        def t3(tensor):
            cx.wait(tensor, WC_DONE, ("GV", kv_cast_ms), *p2_end)
            for kind, t in pe_order:
                if kind == "PAT":
                    cx.wait(tensor, ("GS", pm3[t]))
                    if t >= 1:
                        cx.wait(tensor, ("EVV", ms3v[("NRM", t - 1)] - 3))
                    tensor.matmul(tpf[0:2, 0:512], pat_bf[:, :],
                                  qk_slot(t)[:, 0:512], start=True, stop=True)
                    inst = tensor.matmul(PS_SP[0:2, 0:512], pat_bf[:, :],
                                         qk_slot(t)[:, 512:1024],
                                         start=True, stop=True)
                elif kind == "BC":
                    cx.wait(tensor, ("EVV", ms3v[("NRM", t)]))
                    if t >= 1:
                        cx.wait(tensor, ("EVV", ms3v[("RBE", t - 1)]))
                    for hh in range(2):
                        c0 = slice(512 * hh, 512 * (hh + 1))
                        inst = tensor.matmul(PS_A[:, c0], patT_bf[:, :],
                                             rbf_slot(t)[:, c0],
                                             start=True, stop=True)
                else:
                    w = [qfills[t]["evac_ms"]]
                    if t >= 2:
                        w.append(("EVV", ms3v[("QKE", t - 2)]))
                    cx.wait(tensor, *w)
                    ps = ps3(t)
                    co = slice(64 * t, 64 * t + 64)
                    for hh in range(2):
                        c0 = slice(512 * hh, 512 * (hh + 1))
                        tensor.matmul(ps[0:64, c0], kv_bf[0:64, co],
                                      q_t(t)[0:64, c0], start=True, stop=True)
                        inst = tensor.matmul(ps[64:128, c0],
                                             kv_bf[64:128, co],
                                             q_t(t)[64:128, c0],
                                             start=True, stop=True)
                assert cx.inc(inst, "MM") == ms3mm[(kind, t)]
        blk.tensor(t3)

        def v3(vector):
            cx.wait(vector, ("KVB", 16))
            inst = vector.tensor_copy(kv_bf[:, :], SCR[:, 0:1024])
            assert cx.inc(inst, "GV") == kv_cast_ms
            for kind, t in v_order:
                if kind == "NRM":
                    base = ms3v[("NRM", t)] - 5
                    cx.wait(vector, ("MM", ms3mm[("PAT", t)]))
                    if t >= 2:
                        cx.wait(vector, ("MM", ms3mm[("BC", t - 2)]))
                    a = vector.tensor_copy(nsb_slot(t)[:, 0:512],
                                           tpf[0:2, 0:512])
                    assert cx.inc(a, "EVV") == base + 1
                    a = vector.tensor_copy(nsb_slot(t)[:, 512:1024],
                                           PS_SP[0:2, 0:512])
                    assert cx.inc(a, "EVV") == base + 2
                    vector.wait_ge(sems["EVV"], base + 2)
                    a = vector.tensor_scalar_max(nsb_slot(t), nsb_slot(t),
                                                 1e-6)
                    assert cx.inc(a, "EVV") == base + 3
                    vector.wait_ge(sems["EVV"], base + 3)
                    a = vector.reciprocal(rf_slot(t), nsb_slot(t))
                    assert cx.inc(a, "EVV") == base + 4
                    vector.wait_ge(sems["EVV"], base + 4)
                    a = vector.tensor_copy(rbf_slot(t), rf_slot(t))
                    assert cx.inc(a, "EVV") == base + 5
                elif kind == "RBE":
                    cx.wait(vector, ("MM", ms3mm[("BC", t)]))
                    if t >= 2:
                        cx.wait(vector, ("EVV", ms3v[("QKE", t - 2)]))
                    a = vector.tensor_copy(rb_slot(t), PS_A[:, :])
                    assert cx.inc(a, "EVV") == ms3v[("RBE", t)]
                else:
                    cx.wait(vector, ("MM", ms3mm[("QKV", t)]),
                            ("EVV", ms3v[("RBE", t)]))
                    a = vector.tensor_mul(q_t(t), ps3(t)[:, :], rb_slot(t))
                    assert cx.inc(a, "EVV") == ms3v[("QKE", t)]
        blk.vector(v3)
        p3_end = [("EVV", cx.c["EVV"]), ("MM", cx.c["MM"]),
                  ("EVS", cx.c["EVS"])]

        pipe4 = Pipeline(cx, psums, barrier=p3_end)
        wofills = []
        for gi in range(NT):
            q, dt = gi // 4, gi % 4

            def wo_evac(scalar, psum, gi=gi):
                tdst = SCR[:, 0:1024] if gi % 2 == 0 else SCR[:, 1024:2048]
                return scalar.activation(tdst, psum[:, :], AF.Identity,
                                         bias=bo_sb[:, gi:gi + 1])

            def wo_post(vector, gi=gi):
                tsrc = SCR[:, 0:1024] if gi % 2 == 0 else SCR[:, 1024:2048]
                c = cx.inc(vector.tensor_add(res_t(gi), res_t(gi), tsrc),
                           "VC")
                vector.wait_ge(sems["VC"], c)
                return vector.tensor_copy(h_t(gi), res_t(gi))
            wofills.append(pipe4.add(
                "A" if gi % 2 == 0 else "B", proj_fill_mm(f"wo{q}", dt, q_t),
                "S", wo_evac,
                pe_waits=[wms[f"wo{q}"], WC_DONE],
                post=wo_post, post_waits=[XR_DONE],
                tag=f"wo{gi}"))
        pipe4.plan()
        for gi in range(2, NT):
            wofills[gi]["evac_waits"].append(wofills[gi - 2]["post_ms"])
        for q in range(4):
            ring_consumed[f"wo{q}"] = wofills[4 * q + 3]["mm_ms"]
        p4_end = pipe4.emit(blk)

        g_sq2 = []

        def v5a(vector):
            cx.wait(vector, *p4_end)
            for i in range(NT):
                inst = vector.tensor_mul(q_t(i), h_t(i), h_t(i))
                g_sq2.append(cx.inc(inst, "GV"))
        blk.vector(v5a)
        ms5, h2_done = ln_stats_and_h(g_sq2)

        ms6 = {}

        def c6_logits_pe(tensor):
            for hh in range(2):
                c = slice(512 * hh, 512 * (hh + 1))
                for s in range(NT):
                    inst = tensor.matmul(PS_KV[0:32, c],
                                         gate_bf[:, 32 * s:32 * (s + 1)],
                                         h_t(s)[:, c],
                                         start=(s == 0), stop=(s == NT - 1))
            assert cx.inc(inst, "MM") == ms6["logits"]

        def c6_logits_v(vector):
            cx.wait(vector, ("MM", ms6["logits"]))
            inst = vector.tensor_copy(SCR[0:32, 0:1024], PS_KV[0:32, :])
            assert cx.inc(inst, "GV") == ms6["lg"]

        def c6_tr_pe(tensor):
            cx.wait(tensor, ("GV", ms6["lg"]), WC_DONE)
            for b in range(8):
                inst = tensor.transpose(
                    tpf[:, 32 * b:32 * (b + 1)],
                    SCR[0:32, 128 * b:128 * (b + 1)], idf[0:32, 0:32])
            assert cx.inc(inst, "MM") == ms6["tr"]

        def c6_tr_v(vector):
            cx.wait(vector, ("MM", ms6["tr"]))
            inst = vector.tensor_copy(probs_tm[:, 0:256], tpf[:, 0:256])
            assert cx.inc(inst, "GV") == ms6["trv"]

        def c6_soft_s(scalar):
            cx.wait(scalar, ("GV", ms6["trv"]))
            inst = scalar.activation(rw_tm[:, 0:256], probs_tm[:, 0:256],
                                     AF.Exp)
            assert cx.inc(inst, "GS") == ms6["exp"]

        def c6_soft_v(vector):
            vcb = {}

            def step(b, inst):
                vcb[b] = cx.inc(inst, "VC")

            def dep(b):
                if b in vcb:
                    vector.wait_ge(sems["VC"], vcb[b])
            cx.wait(vector, ("GS", ms6["exp"]))

            def eb(b):
                return rw_tm[:, 32 * b:32 * (b + 1)]

            def pb(b):
                return probs_tm[:, 32 * b:32 * (b + 1)]

            def t8(b):
                return top8[:, 8 * b:8 * b + 8]
            for b in range(8):
                step(b, vector.reduce_sum(scol[:, b:b + 1], eb(b), axis=AX.X))
            for b in range(8):
                dep(b)
                step(b, vector.reciprocal(rcol[:, b:b + 1],
                                          scol[:, b:b + 1]))
            for b in range(8):
                dep(b)
                step(b, vector.tensor_scalar_mul(pb(b), eb(b),
                                                 rcol[:, b:b + 1]))
            for b in range(8):
                dep(b)
                step(b, vector.max(out=t8(b), in_=pb(b)))
            for b in range(8):
                dep(b)
                step(b, vector.reduce_sum(s4col[:, b:b + 1], t8(b)[:, 0:4],
                                          axis=AX.X))
            for b in range(8):
                dep(b)
                step(b, vector.reciprocal(r4col[:, b:b + 1],
                                          s4col[:, b:b + 1]))
            for b in range(8):
                dep(b)
                step(b, vector.tensor_scalar(eb(b), pb(b), t8(b)[:, 3:4],
                                             None, op0=ALU.is_ge))
            for b in range(8):
                dep(b)
                step(b, vector.tensor_mul(eb(b), eb(b), pb(b)))
            for b in range(8):
                dep(b)
                inst = vector.tensor_scalar_mul(eb(b), eb(b),
                                                r4col[:, b:b + 1])
                assert cx.inc(inst, "GV") == ms6[f"rw{b}"]

        def c6_rt_pe(tensor):
            cx.wait(tensor, ("GV", ms6["rw7"]), ("GV", ms6["trv"]))
            for b in range(8):
                dst = (tpf[0:32, 128 * b:128 * (b + 1)] if b < 4 else
                       PS_SP[0:32, 128 * (b - 4):128 * (b - 3)])
                inst = tensor.transpose(dst, rw_tm[:, 32 * b:32 * (b + 1)],
                                        idf[:, :])
            assert cx.inc(inst, "MM") == ms6["rt"]

        def c6_rt_v(vector):
            def step(inst):
                c = cx.inc(inst, "VC")
                vector.wait_ge(sems["VC"], c)
            cx.wait(vector, ("MM", ms6["rt"]))
            step(vector.tensor_copy(SCR[0:32, 1024:1536], tpf[0:32, 0:512]))
            step(vector.tensor_copy(SCR[0:32, 1536:2048],
                                    PS_SP[0:32, 0:512]))
            inst = vector.tensor_copy(rw_em_bf[:, :], SCR[0:32, 1024:2048])
            assert cx.inc(inst, "GV") == ms6["rw_bf"]

        pipe7 = Pipeline(cx, psums,
                         barrier=[("GV", h2_done[-1]),
                                  ("GV", ms5["bc_ev"])])
        gfills = {}
        ufills = {}
        dfills = {}

        def gup_mm(e, t, dt):
            def mm(tensor, psum):
                wn = f"g{e}_{t}"
                for s in range(NT):
                    lo = 512 * s + 128 * dt
                    st, sp = (s == 0), (s == NT - 1)
                    for hh in range(2):
                        c = slice(512 * hh, 512 * (hh + 1))
                        inst = tensor.matmul(
                            psum[:, c],
                            slot_ap(ring_idx[wn], 8192)[:, lo:lo + 128],
                            h_t(s)[:, c], start=st, stop=sp)
                return inst
            return mm

        def dwn_mm(e, hf, dt):
            def mm(tensor, psum):
                wn = f"d{e}_{hf}"
                for s in range(NI):
                    lo = 1024 * s + 128 * dt
                    st, sp = (s == 0), (s == NI - 1)
                    for hh in range(2):
                        c = slice(512 * hh, 512 * (hh + 1))
                        inst = tensor.matmul(
                            psum[:, c],
                            slot_ap(ring_idx[wn], 6144)[:, lo:lo + 128],
                            q_t(6 + s)[:, c], start=st, stop=sp)
                return inst
            return mm

        nab = [0]

        def abkey():
            nab[0] += 1
            return "A" if nab[0] % 2 == 1 else "B"

        c6_entries = {}
        gpost_ms = {}

        def make_gpost(e):
            def gpost_v(vector):
                rbw = ("RB0" if e % 2 == 0 else "RB1", 16 * (e // 2 + 1))
                src = rwb_bf[:, 1024 * (e % 2):1024 * (e % 2 + 1)]
                for j in range(6):
                    cx.wait(vector, gfills[(e, j)]["evac_ms"], rbw)
                    inst = vector.tensor_mul(q_t(j), q_t(j), src)
                    assert cx.inc(inst, "GV") == gpost_ms[(e, j)]
            return gpost_v

        for e in range(NE):
            for t in range(3):
                for dt in range(4):
                    j = 4 * t + dt
                    if j < 6:
                        def g_evac(scalar, psum, j=j):
                            return scalar.activation(q_t(j), psum[:, :],
                                                     AF.Silu)
                        gfills[(e, j)] = pipe7.add(
                            abkey(), gup_mm(e, t, dt), "S", g_evac,
                            pe_waits=[wms[f"g{e}_{t}"]],
                            tag=f"g{e}_{j}")
                        if e == 0 and j == 0:
                            c6_entries["logits"] = pipe7.add_custom(
                                pe=c6_logits_pe, v=c6_logits_v, mm_incs=1)
                        elif e == 0 and j == 1:
                            c6_entries["tr"] = pipe7.add_custom(
                                pe=c6_tr_pe, v=c6_tr_v, mm_incs=1)
                        elif e == 0 and j == 2:
                            c6_entries["soft"] = pipe7.add_custom(
                                s=c6_soft_s, v=c6_soft_v)
                        elif e == 0 and j == 4:
                            c6_entries["rt"] = pipe7.add_custom(
                                pe=c6_rt_pe, v=c6_rt_v, mm_incs=1)
                        if j == 5:
                            pipe7.add_custom(v=make_gpost(e))
                    else:
                        ju = j - 6

                        def u_evac(vector, psum, ju=ju):
                            return vector.tensor_mul(q_t(6 + ju), psum[:, :],
                                                     q_t(ju))
                        ufills[(e, ju)] = pipe7.add(
                            abkey(), gup_mm(e, t, dt), "V", u_evac,
                            pe_waits=[wms[f"g{e}_{t}"]],
                            evac_waits=[], tag=f"u{e}_{ju}")
            for hf in range(2):
                for dt in range(8):
                    gi = 8 * hf + dt

                    def d_evac(vector, psum, gi=gi):
                        return vector.tensor_add(res_t(gi), res_t(gi),
                                                 psum[:, :])
                    dfills[(e, gi)] = pipe7.add(
                        abkey(), dwn_mm(e, hf, dt), "V", d_evac,
                        pe_waits=[wms[f"d{e}_{hf}"]], tag=f"d{e}_{gi}")
        pipe7.plan()
        ms6["logits"] = c6_entries["logits"]["mm_end"]
        ms6["tr"] = c6_entries["tr"]["mm_end"]
        ms6["rt"] = c6_entries["rt"]["mm_end"]
        _g0 = cx.c["GV"]
        ms6["lg"] = _g0 + 1
        ms6["trv"] = _g0 + 2
        for b in range(8):
            ms6[f"rw{b}"] = _g0 + 3 + b
        ms6["rw_bf"] = _g0 + 11
        for e in range(NE):
            for j in range(6):
                gpost_ms[(e, j)] = _g0 + 11 + 6 * e + j + 1
        ms6["exp"] = cx.c["GS"] + 1
        for e in range(NE):
            for ju in range(6):
                ufills[(e, ju)]["evac_waits"].append(
                    ("GV", gpost_ms[(e, ju)]))
            for gi in range(NT):
                dfills[(e, gi)]["pe_waits"].append(
                    ufills[(e, 5)]["evac_ms"])
            if e > 0:
                for j in range(6):
                    gfills[(e, j)]["evac_waits"].append(
                        ufills[(e - 1, j)]["evac_ms"])
            ring_consumed[f"g{e}_0"] = gfills[(e, 3)]["mm_ms"]
            ring_consumed[f"g{e}_1"] = ufills[(e, 1)]["mm_ms"]
            ring_consumed[f"g{e}_2"] = ufills[(e, 5)]["mm_ms"]
            ring_consumed[f"d{e}_0"] = dfills[(e, 7)]["mm_ms"]
            ring_consumed[f"d{e}_1"] = dfills[(e, 15)]["mm_ms"]
        pipe7.emit(blk)

        def sync3(sync):
            cx.wait(sync, ("GV", ms6["rw_bf"]))
            inst = sync.dma_start(out=rw_dram, in_=rw_em_bf[:, :])
            cx.inc(inst, "RWS", 16)
            sync.wait_ge(sems["RWS"], 16)
            for e in range(NE):
                key = "RB0" if e % 2 == 0 else "RB1"
                if e >= 2:
                    cx.wait(sync, ("GV", gpost_ms[(e - 2, 5)]),
                            (key, 16 * (e // 2)))
                dst = rwb_bf[:, 1024 * (e % 2):1024 * (e % 2 + 1)]
                inst = sync.dma_start(
                    out=dst,
                    in_=rw_dram[e:e + 1, :].broadcast_to([128, 1024]))
                cx.inc(inst, key, 16)
        blk.sync(sync3)

        def sync_out(sync):
            for gi in range(NT):
                cx.wait(sync, dfills[(NE - 1, gi)]["evac_ms"])
                inst = sync.dma_start(out=out[128 * gi:128 * (gi + 1), :],
                                      in_=res_t(gi))
                cx.inc(inst, "ST", 16)
            cx.wait(sync, ("ST", cx.c["ST"]))
        blk.sync(sync_out)

        def gp(gpsimd):
            for name, kind, info, key in wplan:
                if kind == "cc":
                    cx.wait(gpsimd, XS_DONE)
                    inst = gpsimd.collective_compute(
                        "AllReduce", ALU.add,
                        replica_groups=[list(range(NCORES))],
                        ins=[cc_in[:, :].opt()], outs=[cc_out[:, :].opt()])
                    cx.inc(inst, "CC", 1)
                    continue
                if kind == "plain":
                    dst, src = info
                    inst = gpsimd.dma_start(out=dst, in_=src)
                else:
                    idx, width, src = info
                    if idx >= 2:
                        prev = ring_entries[idx - 2]
                        cx.wait(gpsimd, ("MM", ring_consumed[prev]),
                                (key, 16 * (idx // 2)))
                    inst = gpsimd.dma_start(out=slot_ap(idx, width), in_=src)
                got = cx.inc(inst, key, 16)
                assert (key, got) == wms[name], (name, key, got, wms[name])
        blk.gpsimd(gp)

    return nc


def _pack_weights(inputs):
    f = np.float32

    def attn(w):
        return np.ascontiguousarray(
            np.asarray(w, f).reshape(16, 128, 4, 512).transpose(2, 1, 0, 3)
        ).reshape(4, 128, 8192)

    d = {}
    d["wq"] = attn(inputs["L0_attn_Wq"])
    d["wk"] = attn(inputs["L0_attn_Wk"])
    d["wv"] = attn(inputs["L0_attn_Wv"])
    d["wo"] = attn(inputs["L0_attn_Wo"])
    d["boc"] = np.ascontiguousarray(
        np.asarray(inputs["L0_attn_bo"], f).reshape(16, 128).T)
    d["gatep"] = np.ascontiguousarray(
        np.asarray(inputs["L0_ffn_gate"], f).reshape(16, 128, 32)
        .transpose(1, 0, 2)).reshape(128, 512)
    g = np.asarray(inputs["L0_ffn_gate_up_proj"], f)
    d["gup"] = np.ascontiguousarray(
        g.reshape(32, 16, 128, 3, 512).transpose(0, 3, 2, 1, 4)
    ).reshape(32, 3, 128, 8192)
    dn = np.asarray(inputs["L0_ffn_down_proj"], f)
    d["dwn"] = np.ascontiguousarray(
        dn.reshape(32, 6, 128, 2, 1024).transpose(0, 3, 2, 1, 4)
    ).reshape(32, 2, 128, 6144)
    d["cid"] = np.eye(128, dtype=f)
    pat = np.zeros((128, 2), f)
    pat[:64, 0] = 1.0
    pat[64:, 1] = 1.0
    d["cpat"] = pat
    d["cpatT"] = np.ascontiguousarray(pat.T)
    d["conec"] = np.ones((128, 1), f)
    d["coner"] = np.ones((1, 128), f)
    return d


def kernel(L0_attn_hidden_states, L0_attn_Wq, L0_attn_Wk, L0_attn_Wv,
           L0_attn_Wo, L0_attn_bo, L0_ffn_gate, L0_ffn_gate_up_proj,
           L0_ffn_down_proj):
    shared = _pack_weights(dict(
        L0_attn_Wq=L0_attn_Wq, L0_attn_Wk=L0_attn_Wk, L0_attn_Wv=L0_attn_Wv,
        L0_attn_Wo=L0_attn_Wo, L0_attn_bo=L0_attn_bo, L0_ffn_gate=L0_ffn_gate,
        L0_ffn_gate_up_proj=L0_ffn_gate_up_proj,
        L0_ffn_down_proj=L0_ffn_down_proj))
    x = np.asarray(L0_attn_hidden_states, np.float32).reshape(8192, 2048)
    in_maps = []
    for c in range(NCORES):
        xc = np.ascontiguousarray(x[1024 * c:1024 * (c + 1), :].T)
        m = dict(shared)
        m["xT"] = xc.reshape(16, 128, 1024)
        in_maps.append(m)
    nc = build_nc()
    res = run_bass_kernel_spmd(nc, in_maps, core_ids=list(range(NCORES)))
    outp = np.empty((8192, 2048), np.float32)
    for c in range(NCORES):
        outp[1024 * c:1024 * (c + 1), :] = res.results[c]["out"].T
    return outp.reshape(1, 8192, 2048)


# version 06 (final)
# speedup vs baseline: 1.0436x; 1.0209x over previous
from contextlib import ExitStack

import numpy as np
import concourse.bass as bass
import concourse.mybir as mybir
from concourse.bass_utils import run_bass_kernel_spmd

F32 = mybir.dt.float32
BF16 = mybir.dt.bfloat16
AF = mybir.ActivationFunctionType
ALU = mybir.AluOpType
AX = mybir.AxisListType

D = 2048
T = 1024
NT = 16
NE = 32
NI = 6
NCORES = 8
EPS = 1e-5

SEM_KEYS = ("MM", "EVS", "EVV", "TP", "GV", "GS", "CC", "ST", "VC",
            "WC", "WX0", "WX1", "WX2", "WX3", "XR", "XS", "KVB", "KSR",
            "WR0", "WR1", "RWS", "RB0", "RB1")


class Ctx:

    def __init__(self, sems):
        self.sems = sems
        self.c = {k: 0 for k in SEM_KEYS}

    def inc(self, inst, key, n=1):
        inst.then_inc(self.sems[key], n)
        self.c[key] += n
        return self.c[key]

    def wait(self, eng, *specs):
        best = {}
        for spec in specs:
            if callable(spec):
                spec = spec()
            if not spec:
                continue
            key, val = spec
            if val and val > 0:
                best[key] = max(best.get(key, 0), val)
        for key, val in best.items():
            eng.wait_ge(self.sems[key], val)


class Pipeline:

    def __init__(self, cx, psums, barrier=()):
        self.cx = cx
        self.psums = psums
        self.barrier = list(barrier)
        self.entries = []
        self.fills = []

    def add(self, key, mm, evac_eng, evac, pe_waits=(), evac_waits=(),
            post=None, post_waits=(), tag=None, post_gates_psum=True):
        f = dict(kind="fill", key=key, mm=mm, eng=evac_eng, evac=evac,
                 pe_waits=list(pe_waits), evac_waits=list(evac_waits),
                 post=post, post_waits=list(post_waits), tag=tag,
                 post_gates_psum=post_gates_psum)
        self.entries.append(f)
        self.fills.append(f)
        return f

    def add_custom(self, pe=None, v=None, s=None, mm_incs=0):
        f = dict(kind="custom", pe=pe, v=v, s=s, mm_incs=mm_incs)
        self.entries.append(f)
        return f

    def plan(self):
        cx = self.cx
        evs, evv, mm = cx.c["EVS"], cx.c["EVV"], cx.c["MM"]
        last_on = {}
        for f in self.entries:
            if f["kind"] == "custom":
                mm += f["mm_incs"]
                f["mm_end"] = mm
                continue
            mm += 1
            f["mm_ms"] = mm
            if f["eng"] == "S":
                evs += 1
                f["evac_ms"] = ("EVS", evs)
            else:
                evv += 1
                f["evac_ms"] = ("EVV", evv)
            if f["post"] is not None:
                evv += 1
                f["post_ms"] = ("EVV", evv)
            f["prev_same"] = last_on.get(f["key"])
            last_on[f["key"]] = f
        self.planned = dict(MM=mm, EVS=evs, EVV=evv)
        return self

    def emit(self, blk):
        cx, psums = self.cx, self.psums

        def tbody(tensor):
            for f in self.entries:
                if f["kind"] == "custom":
                    if f["pe"]:
                        f["pe"](tensor)
                    continue
                waits = list(f["pe_waits"])
                p = f["prev_same"]
                if p is None:
                    waits += self.barrier
                else:
                    waits.append(p["evac_ms"])
                    if p["post"] is not None and p["post_gates_psum"]:
                        waits.append(p["post_ms"])
                cx.wait(tensor, *waits)
                inst = f["mm"](tensor, psums[f["key"]])
                got = cx.inc(inst, "MM")
                assert got == f["mm_ms"], (got, f["mm_ms"], f["tag"])

        def sbody(scalar):
            for f in self.entries:
                if f["kind"] == "custom":
                    if f["s"]:
                        f["s"](scalar)
                    continue
                if f["eng"] != "S":
                    continue
                cx.wait(scalar, ("MM", f["mm_ms"]), *f["evac_waits"])
                inst = f["evac"](scalar, psums[f["key"]])
                got = cx.inc(inst, "EVS")
                assert ("EVS", got) == f["evac_ms"], f["tag"]

        def vbody(vector):
            for f in self.entries:
                if f["kind"] == "custom":
                    if f["v"]:
                        f["v"](vector)
                    continue
                if f["eng"] == "V":
                    cx.wait(vector, ("MM", f["mm_ms"]), *f["evac_waits"])
                    inst = f["evac"](vector, psums[f["key"]])
                    got = cx.inc(inst, "EVV")
                    assert ("EVV", got) == f["evac_ms"], f["tag"]
                if f["post"] is not None:
                    cx.wait(vector, f["evac_ms"], *f["post_waits"])
                    inst = f["post"](vector)
                    got = cx.inc(inst, "EVV")
                    assert got == f["post_ms"][1], f["tag"]

        blk.tensor(tbody)
        blk.scalar(sbody)
        blk.vector(vbody)
        for k, v in self.planned.items():
            assert cx.c[k] == v, (k, cx.c[k], v)
        return [("MM", self.planned["MM"]), ("EVS", self.planned["EVS"]),
                ("EVV", self.planned["EVV"])]


def build_nc():
    nc = bass.Bass("TRN2", target_bir_lowering=False, debug=False)

    xT = nc.dram_tensor("xT", [NT, 128, T], F32, kind="ExternalInput").ap()
    wq = nc.dram_tensor("wq", [4, 128, 8192], F32, kind="ExternalInput").ap()
    wk = nc.dram_tensor("wk", [4, 128, 8192], F32, kind="ExternalInput").ap()
    wv = nc.dram_tensor("wv", [4, 128, 8192], F32, kind="ExternalInput").ap()
    wo = nc.dram_tensor("wo", [4, 128, 8192], F32, kind="ExternalInput").ap()
    boc = nc.dram_tensor("boc", [128, 16], F32, kind="ExternalInput").ap()
    gatep = nc.dram_tensor("gatep", [128, 512], F32, kind="ExternalInput").ap()
    gup = nc.dram_tensor("gup", [NE, 3, 128, 8192], F32,
                         kind="ExternalInput").ap()
    dwn = nc.dram_tensor("dwn", [NE, 2, 128, 6144], F32,
                         kind="ExternalInput").ap()
    cid = nc.dram_tensor("cid", [128, 128], F32, kind="ExternalInput").ap()
    cpat = nc.dram_tensor("cpat", [128, 2], F32, kind="ExternalInput").ap()
    cpatT = nc.dram_tensor("cpatT", [2, 128], F32, kind="ExternalInput").ap()
    conec = nc.dram_tensor("conec", [128, 1], F32, kind="ExternalInput").ap()
    coner = nc.dram_tensor("coner", [1, 128], F32, kind="ExternalInput").ap()
    out = nc.dram_tensor("out", [D, T], F32, kind="ExternalOutput").ap()
    cc_in = nc.dram_tensor("cc_in", [128, 1040], F32)
    cc_out = nc.dram_tensor("cc_out", [128, 1040], F32)
    rw_dram = nc.dram_tensor("rw_dram", [32, 1024], BF16).ap()

    with ExitStack() as ctx:
        E = ctx.enter_context
        AR_RES = E(nc.sbuf_tensor("AR_RES", [128, NT * T], F32))
        AR_H = E(nc.sbuf_tensor("AR_H", [128, NT * T], BF16))
        AR_Q = E(nc.sbuf_tensor("AR_Q", [128, NT * T], BF16))
        AR_W = E(nc.sbuf_tensor("AR_W", [128, 16384], BF16))
        SCR = E(nc.sbuf_tensor("SCR", [128, 2048], F32))
        SROW_A = E(nc.sbuf_tensor("SROW_A", [1, T], F32))
        SROW_B = E(nc.sbuf_tensor("SROW_B", [1, T], F32))
        n_sb = E(nc.sbuf_tensor("n_sb", [34, T], F32))
        r_f = E(nc.sbuf_tensor("r_f", [34, T], F32))
        r_bf = E(nc.sbuf_tensor("r_bf", [2, 2 * T], BF16))
        qk_t = E(nc.sbuf_tensor("qk_t", [128, 2 * T], BF16))
        kv_bf = E(nc.sbuf_tensor("kv_bf", [128, 1024], BF16))
        keysum = E(nc.sbuf_tensor("keysum", [128, 16], F32))
        probs_tm = E(nc.sbuf_tensor("probs_tm", [128, 256], F32))
        rw_tm = E(nc.sbuf_tensor("rw_tm", [128, 256], F32))
        rw_em_bf = E(nc.sbuf_tensor("rw_em_bf", [32, 1024], BF16))
        top8 = E(nc.sbuf_tensor("top8", [128, 64], F32))
        scol = E(nc.sbuf_tensor("scol", [128, 8], F32))
        rcol = E(nc.sbuf_tensor("rcol", [128, 8], F32))
        s4col = E(nc.sbuf_tensor("s4col", [128, 8], F32))
        r4col = E(nc.sbuf_tensor("r4col", [128, 8], F32))
        rwb_bf = qk_t
        gate_bf = E(nc.sbuf_tensor("gate_bf", [128, 512], BF16))
        bo_sb = E(nc.sbuf_tensor("bo_sb", [128, 16], F32))
        idf = E(nc.sbuf_tensor("idf", [128, 128], F32))
        idb = E(nc.sbuf_tensor("idb", [128, 128], BF16))
        pat_bf = E(nc.sbuf_tensor("pat_bf", [128, 2], BF16))
        patT_bf = E(nc.sbuf_tensor("patT_bf", [2, 128], BF16))
        onec_bf = E(nc.sbuf_tensor("onec_bf", [128, 1], BF16))
        oner_f = E(nc.sbuf_tensor("oner_f", [1, 128], F32))
        PS_A = E(nc.psum_tensor("PS_A", [128, 1024], F32))
        PS_B = E(nc.psum_tensor("PS_B", [128, 1024], F32))
        PS_KV = E(nc.psum_tensor("PS_KV", [128, 1024], F32))
        PS_TP = E(nc.psum_tensor("PS_TP", [128, 1024], BF16))
        PS_SP = E(nc.psum_tensor("PS_SP", [128, 512], F32))
        sems = {k: E(nc.semaphore("s" + k)) for k in SEM_KEYS}
        blk = E(nc.Block())

        cx = Ctx(sems)
        psums = {"A": PS_A, "B": PS_B, "KV": PS_KV}
        arb = AR_RES.ap().bitcast(BF16)
        tpf = PS_TP.ap().bitcast(F32)

        def h_t(i):
            return AR_H[:, T * i:T * (i + 1)]

        def q_t(i):
            return AR_Q[:, T * i:T * (i + 1)]

        def res_t(i):
            return AR_RES[:, T * i:T * (i + 1)]

        def kq_t(dt):
            return arb[:, T * dt:T * (dt + 1)]

        def vq_t(dt):
            return arb[:, 4096 + T * dt:4096 + T * (dt + 1)]

        ktp = arb[:, 8192:9216]
        vtp = arb[:, 9216:10240]

        wplan = []
        wms = {}
        dcount = {k: 0 for k in SEM_KEYS}

        def plan_dma(name, key):
            dcount[key] += 16
            wms[name] = (key, dcount[key])
            return wms[name]

        def add_plain(name, dst, src, key):
            wplan.append((name, "plain", (dst, src), key))
            plan_dma(name, key)

        ring_entries = []

        def slot_ap(idx, width):
            base = 8192 * (idx % 2)
            return AR_W[:, base:base + width]

        def add_ring(name, src, width=8192):
            idx = len(ring_entries)
            ring_entries.append(name)
            key = "WR0" if idx % 2 == 0 else "WR1"
            wplan.append((name, "ring", (idx, width, src), key))
            plan_dma(name, key)
            return idx

        for nm, dst, src in [("idb", idb[:, :], cid),
                             ("pat_bf", pat_bf[:, :], cpat),
                             ("patT_bf", patT_bf[:, :], cpatT),
                             ("gate_bf", gate_bf[:, :], gatep),
                             ("onec_bf", onec_bf[:, :], conec),
                             ("idf", idf[:, :], cid),
                             ("bo_sb", bo_sb[:, :], boc),
                             ("oner_f", oner_f[:, :], coner)]:
            add_plain(nm, dst, src, "WC")
        for i in range(NT):
            add_plain(f"x{i}", h_t(i), xT[i, :, :], f"WX{i // 4}")
        WC_DONE = ("WC", dcount["WC"])
        WX_DONE = [(f"WX{g}", dcount[f"WX{g}"]) for g in range(4)]
        ring_idx = {}
        for q in range(4):
            ring_idx[f"wk{q}"] = add_ring(f"wk{q}", wk[q, :, :])
            ring_idx[f"wv{q}"] = add_ring(f"wv{q}", wv[q, :, :])
        for q in range(2):
            ring_idx[f"wq{q}"] = add_ring(f"wq{q}", wq[q, :, :])
        wplan.append(("__collective__", "cc", None, None))
        for q in range(2, 4):
            ring_idx[f"wq{q}"] = add_ring(f"wq{q}", wq[q, :, :])
        for q in range(4):
            ring_idx[f"wo{q}"] = add_ring(f"wo{q}", wo[q, :, :])
        for e in range(NE):
            for t in range(3):
                ring_idx[f"g{e}_{t}"] = add_ring(f"g{e}_{t}", gup[e, t, :, :])
            for hf in range(2):
                ring_idx[f"d{e}_{hf}"] = add_ring(f"d{e}_{hf}",
                                                  dwn[e, hf, :, :], width=6144)
        ring_consumed = {}

        def proj_fill_mm(wname, dt, src_tile_fn):
            def mm(tensor, psum):
                for s in range(NT):
                    lo = 512 * s + 128 * dt
                    st, sp = (s == 0), (s == NT - 1)
                    for hh in range(2):
                        c = slice(512 * hh, 512 * (hh + 1))
                        inst = tensor.matmul(
                            psum[:, c],
                            slot_ap(ring_idx[wname], 8192)[:, lo:lo + 128],
                            src_tile_fn(s)[:, c], start=st, stop=sp)
                return inst
            return mm

        g_sq = []

        def v0a(vector):
            for i in range(NT):
                if i % 4 == 0:
                    cx.wait(vector, WX_DONE[i // 4])
                inst = vector.tensor_mul(q_t(i), h_t(i), h_t(i))
                g_sq.append(cx.inc(inst, "GV"))
        blk.vector(v0a)

        def ln_stats_and_h(sq_ready):
            ms = {}

            def t_stats(tensor):
                cx.wait(tensor, WC_DONE)
                for i in range(NT):
                    cx.wait(tensor, ("GV", sq_ready[i]))
                    st, sp = (i == 0), (i == NT - 1)
                    for hh in range(2):
                        c = slice(512 * hh, 512 * (hh + 1))
                        tensor.matmul(PS_A[0:1, c], onec_bf[:, :],
                                      h_t(i)[:, c], start=st, stop=sp)
                        inst = tensor.matmul(PS_B[0:1, c], onec_bf[:, :],
                                             q_t(i)[:, c], start=st, stop=sp)
                ms["stats"] = cx.inc(inst, "MM")
            blk.tensor(t_stats)

            def v_rows(vector):
                cx.wait(vector, ("MM", ms["stats"]))

                def step(inst):
                    c = cx.inc(inst, "VC")
                    vector.wait_ge(sems["VC"], c)
                step(vector.tensor_copy(SROW_A[:, :], PS_A[0:1, :]))
                step(vector.tensor_copy(SROW_B[:, :], PS_B[0:1, :]))
                step(vector.tensor_scalar_mul(SROW_A[:, :], SROW_A[:, :],
                                              1.0 / D))
                step(vector.tensor_scalar_mul(SROW_B[:, :], SROW_B[:, :],
                                              1.0 / D))
                step(vector.tensor_mul(n_sb[0:1, 0:T], SROW_A[:, :],
                                       SROW_A[:, :]))
                step(vector.tensor_sub(SROW_B[:, :], SROW_B[:, :],
                                       n_sb[0:1, 0:T]))
                inst = vector.tensor_scalar_add(SROW_B[:, :], SROW_B[:, :],
                                                EPS)
                ms["var"] = cx.inc(inst, "GV")
            blk.vector(v_rows)

            def s_rows(scalar):
                cx.wait(scalar, ("GV", ms["var"]))
                inst = scalar.activation(n_sb[0:1, 0:T], SROW_B[:, :], AF.Ln)
                m1 = cx.inc(inst, "GS")
                cx.wait(scalar, ("GS", m1))
                inst = scalar.activation(SROW_B[:, :], n_sb[0:1, 0:T],
                                         AF.Exp, scale=-0.5)
                ms["s"] = cx.inc(inst, "GS")
            blk.scalar(s_rows)

            def v_msrow(vector):
                cx.wait(vector, ("GS", ms["s"]))
                inst = vector.tensor_mul(SROW_A[:, :], SROW_A[:, :],
                                         SROW_B[:, :])
                ms["msrow"] = cx.inc(inst, "GV")
            blk.vector(v_msrow)

            def t_bcast(tensor):
                cx.wait(tensor, ("GV", ms["msrow"]), WC_DONE)
                for hh in range(2):
                    c = slice(512 * hh, 512 * (hh + 1))
                    tensor.matmul(PS_A[:, c], oner_f[:, :], SROW_B[0:1, c],
                                  start=True, stop=True)
                    inst = tensor.matmul(PS_B[:, c], oner_f[:, :],
                                         SROW_A[0:1, c], start=True, stop=True)
                ms["bc"] = cx.inc(inst, "MM")
            blk.tensor(t_bcast)

            h_done = []

            def v_apply(vector):
                cx.wait(vector, ("MM", ms["bc"]))
                for i in range(NT):
                    c = cx.inc(vector.tensor_mul(h_t(i), h_t(i), PS_A[:, :]),
                               "VC")
                    vector.wait_ge(sems["VC"], c)
                    inst = vector.tensor_sub(h_t(i), h_t(i), PS_B[:, :])
                    h_done.append(cx.inc(inst, "GV"))
                ms["bc_ev"] = h_done[-1]
            blk.vector(v_apply)
            return ms, h_done

        ms0, h_done = ln_stats_and_h(g_sq)

        ph1_barrier = [("GV", h_done[-1]), ("GV", ms0["bc_ev"])]
        pipe1 = Pipeline(cx, psums, barrier=ph1_barrier)
        tp0 = cx.c["TP"]
        kv_ms = {}
        kfills, vfills = [], []
        for q in range(4):
            for dt in range(4):
                gi = 4 * q + dt

                def k_evac(scalar, psum, dt=dt):
                    return scalar.activation(kq_t(dt), psum[:, :], AF.Relu)

                def k_post(vector, gi=gi, dt=dt):
                    return vector.reduce_sum(keysum[:, gi:gi + 1], kq_t(dt),
                                             axis=AX.X)
                kfills.append(pipe1.add(
                    "A" if gi % 2 == 0 else "B", proj_fill_mm(f"wk{q}", dt, h_t),
                    "S", k_evac, pe_waits=[wms[f"wk{q}"]],
                    post=k_post, tag=f"k{gi}"))
            for dt in range(4):
                gi = 4 * q + dt

                def v_evac(scalar, psum, dt=dt):
                    return scalar.activation(vq_t(dt), psum[:, :], AF.Identity)
                vfills.append(pipe1.add(
                    "A" if gi % 2 == 0 else "B", proj_fill_mm(f"wv{q}", dt, h_t),
                    "S", v_evac, pe_waits=[wms[f"wv{q}"]], tag=f"v{gi}"))

            def tkv_pe(tensor, q=q):
                if q == 0:
                    cx.wait(tensor, WC_DONE)
                for dt in range(4):
                    gi = 4 * q + dt
                    cx.wait(tensor, ("TP", tp0 + 2 * gi),
                            kfills[gi]["evac_ms"])
                    for b in range(8):
                        inst = tensor.transpose(
                            PS_TP[:, 128 * b:128 * (b + 1)],
                            kq_t(dt)[:, 128 * b:128 * (b + 1)], idb[:, :])
                    kv_ms[f"ktr{gi}"] = cx.inc(inst, "MM")
                    cx.wait(tensor, ("TP", tp0 + 2 * gi + 1),
                            vfills[gi]["evac_ms"])
                    for b in range(8):
                        inst = tensor.transpose(
                            PS_TP[:, 128 * b:128 * (b + 1)],
                            vq_t(dt)[:, 128 * b:128 * (b + 1)], idb[:, :])
                    kv_ms[f"vtr{gi}"] = cx.inc(inst, "MM")
                    cx.wait(tensor, ("TP", tp0 + 2 * gi + 2))
                    co = slice(64 * gi, 64 * gi + 64)
                    for b in range(8):
                        tensor.matmul(PS_KV[0:64, co],
                                      ktp[:, 128 * b:128 * b + 64],
                                      vtp[:, 128 * b:128 * b + 64],
                                      start=(b == 0), stop=(b == 7))
                    for b in range(8):
                        inst = tensor.matmul(
                            PS_KV[64:128, co],
                            ktp[:, 128 * b + 64:128 * b + 128],
                            vtp[:, 128 * b + 64:128 * b + 128],
                            start=(b == 0), stop=(b == 7))
                    kv_ms[f"kvmm{gi}"] = cx.inc(inst, "MM")

            def tkv_v(vector, q=q):
                for dt in range(4):
                    gi = 4 * q + dt
                    cx.wait(vector, ("MM", kv_ms[f"ktr{gi}"]))
                    inst = vector.tensor_copy(ktp, PS_TP[:, :])
                    got = cx.inc(inst, "TP")
                    assert got == tp0 + 2 * gi + 1
                    cx.wait(vector, ("MM", kv_ms[f"vtr{gi}"]))
                    inst = vector.tensor_copy(vtp, PS_TP[:, :])
                    got = cx.inc(inst, "TP")
                    assert got == tp0 + 2 * gi + 2
            pipe1.add_custom(pe=tkv_pe, v=tkv_v, mm_incs=12)
        pipe1.plan()
        for q in range(4):
            ring_consumed[f"wk{q}"] = kfills[4 * q + 3]["mm_ms"]
            ring_consumed[f"wv{q}"] = vfills[4 * q + 3]["mm_ms"]
        p1_end = pipe1.emit(blk)
        kv_done_planned = kv_ms["kvmm15"]

        ms_kvsb = {}

        def v_kvsb(vector):
            cx.wait(vector, ("MM", kv_ms["kvmm15"]))
            inst = vector.tensor_copy(SCR[:, 0:1024], PS_KV[:, :])
            ms_kvsb["v"] = cx.inc(inst, "GV")
        blk.vector(v_kvsb)

        def sync2(sync):
            cx.wait(sync, ("GV", ms_kvsb["v"]))
            inst = sync.dma_start(out=cc_in[:, 0:1024], in_=SCR[:, 0:1024])
            cx.inc(inst, "XS", 16)
            inst = sync.dma_start(out=cc_in[:, 1024:1040], in_=keysum[:, :])
            cx.inc(inst, "XS", 16)
            for i in range(NT):
                inst = sync.dma_start(out=res_t(i), in_=xT[i, :, :])
                cx.inc(inst, "XR", 16)
            sync.wait_ge(sems["CC"], 1)
            inst = sync.dma_start(out=SCR[:, 0:1024], in_=cc_out[:, 0:1024])
            cx.inc(inst, "KVB", 16)
            inst = sync.dma_start(out=keysum[:, :], in_=cc_out[:, 1024:1040])
            cx.inc(inst, "KSR", 16)
        blk.sync(sync2)
        XS_DONE = ("XS", cx.c["XS"])
        XR_DONE = ("XR", cx.c["XR"])

        pipe2 = Pipeline(cx, psums, barrier=p1_end)
        qfills = []
        for gi in range(NT):
            q, dt = gi // 4, gi % 4

            def q_evac(scalar, psum, gi=gi):
                return scalar.activation(q_t(gi), psum[:, :], AF.Relu)
            qfills.append(pipe2.add(
                "A" if gi % 2 == 0 else "B", proj_fill_mm(f"wq{q}", dt, h_t),
                "S", q_evac, pe_waits=[wms[f"wq{q}"]], tag=f"q{gi}"))
        pipe2.plan()
        for q in range(4):
            ring_consumed[f"wq{q}"] = qfills[4 * q + 3]["mm_ms"]
        p2_end = pipe2.emit(blk)

        mm0, evv0, gs0 = cx.c["MM"], cx.c["EVV"], cx.c["GS"]
        pe_order = [("PAT", 0), ("PAT", 1)]
        v_order = [("NRM", 0), ("NRM", 1)]
        s_order = [("PM", 0), ("PM", 1)]
        for t in range(NT):
            pe_order += [("BC", t), ("QKV", t)]
            v_order += [("RBE", t), ("QKE", t)]
            s_order += [("LN3", t), ("EXP3", t)]
            if t + 2 < NT:
                pe_order.append(("PAT", t + 2))
                v_order.append(("NRM", t + 2))
                s_order.append(("PM", t + 2))
        ms3mm = {}
        _c = mm0
        for op in pe_order:
            _c += 1
            ms3mm[op] = _c
        ms3v = {}
        _c = evv0
        for op in v_order:
            _c += {"NRM": 3, "RBE": 1, "QKE": 1}[op[0]]
            ms3v[op] = _c
        ms3s = {}
        _c = gs0
        for op in s_order:
            _c += 1
            ms3s[op] = _c
        kv_cast_ms = cx.c["GV"] + 1

        def qk_slot(t):
            return qk_t[:, 1024 * (t % 2):1024 * (t % 2) + 1024]

        def nsb_slot(t):
            b = 32 * (t % 2)
            return n_sb[b:b + 2, :]

        def rf_slot(t):
            b = 32 * (t % 2)
            return r_f[b:b + 2, :]

        def rbf_slot(t):
            return r_bf[0:2, 1024 * (t % 2):1024 * (t % 2) + 1024]

        def rb_slot(t):
            return SCR[:, 0:1024] if t % 2 == 0 else SCR[:, 1024:2048]

        def ps3(t):
            return PS_KV if t % 2 == 0 else PS_B

        def s3(scalar):
            cx.wait(scalar, ("KSR", 16))
            for kind, t in s_order:
                if kind == "PM":
                    w = [qfills[t]["evac_ms"]]
                    if t >= 2:
                        w.append(("MM", ms3mm[("PAT", t - 2)]))
                    cx.wait(scalar, *w)
                    inst = scalar.activation(qk_slot(t), q_t(t), AF.Identity,
                                             scale=keysum[:, t:t + 1])
                elif kind == "LN3":
                    cx.wait(scalar, ("EVV", ms3v[("NRM", t)]))
                    inst = scalar.activation(rf_slot(t), nsb_slot(t), AF.Ln)
                else:
                    w = [("GS", ms3s[("LN3", t)])]
                    if t >= 2:
                        w.append(("MM", ms3mm[("BC", t - 2)]))
                    cx.wait(scalar, *w)
                    inst = scalar.activation(rbf_slot(t), rf_slot(t), AF.Exp,
                                             scale=-1.0)
                assert cx.inc(inst, "GS") == ms3s[(kind, t)]
        blk.scalar(s3)

        def t3(tensor):
            cx.wait(tensor, WC_DONE, ("GV", kv_cast_ms), *p2_end)
            for kind, t in pe_order:
                if kind == "PAT":
                    cx.wait(tensor, ("GS", ms3s[("PM", t)]))
                    if t >= 1:
                        cx.wait(tensor, ("EVV", ms3v[("NRM", t - 1)] - 1))
                    tensor.matmul(tpf[0:2, 0:512], pat_bf[:, :],
                                  qk_slot(t)[:, 0:512], start=True, stop=True)
                    inst = tensor.matmul(PS_SP[0:2, 0:512], pat_bf[:, :],
                                         qk_slot(t)[:, 512:1024],
                                         start=True, stop=True)
                elif kind == "BC":
                    cx.wait(tensor, ("GS", ms3s[("EXP3", t)]))
                    if t >= 1:
                        cx.wait(tensor, ("EVV", ms3v[("RBE", t - 1)]))
                    for hh in range(2):
                        c0 = slice(512 * hh, 512 * (hh + 1))
                        inst = tensor.matmul(PS_A[:, c0], patT_bf[:, :],
                                             rbf_slot(t)[:, c0],
                                             start=True, stop=True)
                else:
                    w = [qfills[t]["evac_ms"]]
                    if t >= 2:
                        w.append(("EVV", ms3v[("QKE", t - 2)]))
                    cx.wait(tensor, *w)
                    ps = ps3(t)
                    co = slice(64 * t, 64 * t + 64)
                    for hh in range(2):
                        c0 = slice(512 * hh, 512 * (hh + 1))
                        tensor.matmul(ps[0:64, c0], kv_bf[0:64, co],
                                      q_t(t)[0:64, c0], start=True, stop=True)
                        inst = tensor.matmul(ps[64:128, c0],
                                             kv_bf[64:128, co],
                                             q_t(t)[64:128, c0],
                                             start=True, stop=True)
                assert cx.inc(inst, "MM") == ms3mm[(kind, t)]
        blk.tensor(t3)

        def v3(vector):
            cx.wait(vector, ("KVB", 16))
            inst = vector.tensor_copy(kv_bf[:, :], SCR[:, 0:1024])
            assert cx.inc(inst, "GV") == kv_cast_ms
            for kind, t in v_order:
                if kind == "NRM":
                    base = ms3v[("NRM", t)] - 3
                    cx.wait(vector, ("MM", ms3mm[("PAT", t)]))
                    if t >= 2:
                        cx.wait(vector, ("GS", ms3s[("LN3", t - 2)]))
                    a = vector.tensor_copy(nsb_slot(t)[:, 0:512],
                                           tpf[0:2, 0:512])
                    assert cx.inc(a, "EVV") == base + 1
                    a = vector.tensor_copy(nsb_slot(t)[:, 512:1024],
                                           PS_SP[0:2, 0:512])
                    assert cx.inc(a, "EVV") == base + 2
                    vector.wait_ge(sems["EVV"], base + 2)
                    a = vector.tensor_scalar_max(nsb_slot(t), nsb_slot(t),
                                                 1e-6)
                    assert cx.inc(a, "EVV") == base + 3
                elif kind == "RBE":
                    cx.wait(vector, ("MM", ms3mm[("BC", t)]))
                    if t >= 2:
                        cx.wait(vector, ("EVV", ms3v[("QKE", t - 2)]))
                    a = vector.tensor_copy(rb_slot(t), PS_A[:, :])
                    assert cx.inc(a, "EVV") == ms3v[("RBE", t)]
                else:
                    cx.wait(vector, ("MM", ms3mm[("QKV", t)]),
                            ("EVV", ms3v[("RBE", t)]))
                    a = vector.tensor_mul(q_t(t), ps3(t)[:, :], rb_slot(t))
                    assert cx.inc(a, "EVV") == ms3v[("QKE", t)]
        blk.vector(v3)
        p3_end = [("EVV", cx.c["EVV"]), ("MM", cx.c["MM"]),
                  ("EVS", cx.c["EVS"])]

        pipe4 = Pipeline(cx, psums, barrier=p3_end)
        wofills = []
        for gi in range(NT):
            q, dt = gi // 4, gi % 4

            def wo_evac(scalar, psum, gi=gi):
                tdst = SCR[:, 0:1024] if gi % 2 == 0 else SCR[:, 1024:2048]
                return scalar.activation(tdst, psum[:, :], AF.Identity,
                                         bias=bo_sb[:, gi:gi + 1])

            def wo_post(vector, gi=gi):
                tsrc = SCR[:, 0:1024] if gi % 2 == 0 else SCR[:, 1024:2048]
                c = cx.inc(vector.tensor_add(res_t(gi), res_t(gi), tsrc),
                           "VC")
                vector.wait_ge(sems["VC"], c)
                return vector.tensor_copy(h_t(gi), res_t(gi))
            wofills.append(pipe4.add(
                "A" if gi % 2 == 0 else "B", proj_fill_mm(f"wo{q}", dt, q_t),
                "S", wo_evac,
                pe_waits=[wms[f"wo{q}"], WC_DONE],
                post=wo_post, post_waits=[XR_DONE],
                tag=f"wo{gi}"))
        pipe4.plan()
        for gi in range(2, NT):
            wofills[gi]["evac_waits"].append(wofills[gi - 2]["post_ms"])
        for q in range(4):
            ring_consumed[f"wo{q}"] = wofills[4 * q + 3]["mm_ms"]
        p4_end = pipe4.emit(blk)

        g_sq2 = []

        def v5a(vector):
            cx.wait(vector, *p4_end)
            for i in range(NT):
                inst = vector.tensor_mul(q_t(i), h_t(i), h_t(i))
                g_sq2.append(cx.inc(inst, "GV"))
        blk.vector(v5a)
        ms5, h2_done = ln_stats_and_h(g_sq2)

        ms6 = {}

        def c6_logits_pe(tensor):
            for hh in range(2):
                c = slice(512 * hh, 512 * (hh + 1))
                for s in range(NT):
                    inst = tensor.matmul(PS_KV[0:32, c],
                                         gate_bf[:, 32 * s:32 * (s + 1)],
                                         h_t(s)[:, c],
                                         start=(s == 0), stop=(s == NT - 1))
            assert cx.inc(inst, "MM") == ms6["logits"]

        def c6_logits_v(vector):
            cx.wait(vector, ("MM", ms6["logits"]))
            inst = vector.tensor_copy(SCR[0:32, 0:1024], PS_KV[0:32, :])
            assert cx.inc(inst, "GV") == ms6["lg"]

        def c6_tr_pe(tensor):
            cx.wait(tensor, ("GV", ms6["lg"]), WC_DONE)
            for b in range(8):
                inst = tensor.transpose(
                    tpf[:, 32 * b:32 * (b + 1)],
                    SCR[0:32, 128 * b:128 * (b + 1)], idf[0:32, 0:32])
            assert cx.inc(inst, "MM") == ms6["tr"]

        def c6_tr_v(vector):
            cx.wait(vector, ("MM", ms6["tr"]))
            inst = vector.tensor_copy(probs_tm[:, 0:256], tpf[:, 0:256])
            assert cx.inc(inst, "GV") == ms6["trv"]

        def c6_soft_s(scalar):
            cx.wait(scalar, ("GV", ms6["trv"]))
            inst = scalar.activation(rw_tm[:, 0:256], probs_tm[:, 0:256],
                                     AF.Exp)
            assert cx.inc(inst, "GS") == ms6["exp"]

        def c6_soft_v(vector):
            vcb = {}

            def step(b, inst):
                vcb[b] = cx.inc(inst, "VC")

            def dep(b):
                if b in vcb:
                    vector.wait_ge(sems["VC"], vcb[b])
            cx.wait(vector, ("GS", ms6["exp"]))

            def eb(b):
                return rw_tm[:, 32 * b:32 * (b + 1)]

            def pb(b):
                return probs_tm[:, 32 * b:32 * (b + 1)]

            def t8(b):
                return top8[:, 8 * b:8 * b + 8]
            for b in range(8):
                step(b, vector.reduce_sum(scol[:, b:b + 1], eb(b), axis=AX.X))
            for b in range(8):
                dep(b)
                step(b, vector.reciprocal(rcol[:, b:b + 1],
                                          scol[:, b:b + 1]))
            for b in range(8):
                dep(b)
                step(b, vector.tensor_scalar_mul(pb(b), eb(b),
                                                 rcol[:, b:b + 1]))
            for b in range(8):
                dep(b)
                step(b, vector.max(out=t8(b), in_=pb(b)))
            for b in range(8):
                dep(b)
                step(b, vector.reduce_sum(s4col[:, b:b + 1], t8(b)[:, 0:4],
                                          axis=AX.X))
            for b in range(8):
                dep(b)
                step(b, vector.reciprocal(r4col[:, b:b + 1],
                                          s4col[:, b:b + 1]))
            for b in range(8):
                dep(b)
                step(b, vector.tensor_scalar(eb(b), pb(b), t8(b)[:, 3:4],
                                             None, op0=ALU.is_ge))
            for b in range(8):
                dep(b)
                step(b, vector.tensor_mul(eb(b), eb(b), pb(b)))
            for b in range(8):
                dep(b)
                inst = vector.tensor_scalar_mul(eb(b), eb(b),
                                                r4col[:, b:b + 1])
                assert cx.inc(inst, "GV") == ms6[f"rw{b}"]

        def c6_rt_pe(tensor):
            cx.wait(tensor, ("GV", ms6["rw7"]), ("GV", ms6["trv"]))
            for b in range(8):
                dst = (tpf[0:32, 128 * b:128 * (b + 1)] if b < 4 else
                       PS_SP[0:32, 128 * (b - 4):128 * (b - 3)])
                inst = tensor.transpose(dst, rw_tm[:, 32 * b:32 * (b + 1)],
                                        idf[:, :])
            assert cx.inc(inst, "MM") == ms6["rt"]

        def c6_rt_v(vector):
            def step(inst):
                c = cx.inc(inst, "VC")
                vector.wait_ge(sems["VC"], c)
            cx.wait(vector, ("MM", ms6["rt"]))
            step(vector.tensor_copy(SCR[0:32, 1024:1536], tpf[0:32, 0:512]))
            step(vector.tensor_copy(SCR[0:32, 1536:2048],
                                    PS_SP[0:32, 0:512]))
            inst = vector.tensor_copy(rw_em_bf[:, :], SCR[0:32, 1024:2048])
            assert cx.inc(inst, "GV") == ms6["rw_bf"]

        pipe7 = Pipeline(cx, psums,
                         barrier=[("GV", h2_done[-1]),
                                  ("GV", ms5["bc_ev"])])
        gfills = {}
        ufills = {}
        dfills = {}

        def gup_mm(e, t, dt):
            def mm(tensor, psum):
                wn = f"g{e}_{t}"
                for s in range(NT):
                    lo = 512 * s + 128 * dt
                    st, sp = (s == 0), (s == NT - 1)
                    for hh in range(2):
                        c = slice(512 * hh, 512 * (hh + 1))
                        inst = tensor.matmul(
                            psum[:, c],
                            slot_ap(ring_idx[wn], 8192)[:, lo:lo + 128],
                            h_t(s)[:, c], start=st, stop=sp)
                return inst
            return mm

        def dwn_mm(e, hf, dt):
            def mm(tensor, psum):
                wn = f"d{e}_{hf}"
                for s in range(NI):
                    lo = 1024 * s + 128 * dt
                    st, sp = (s == 0), (s == NI - 1)
                    for hh in range(2):
                        c = slice(512 * hh, 512 * (hh + 1))
                        inst = tensor.matmul(
                            psum[:, c],
                            slot_ap(ring_idx[wn], 6144)[:, lo:lo + 128],
                            q_t(6 + s)[:, c], start=st, stop=sp)
                return inst
            return mm

        nab = [0]

        def abkey():
            nab[0] += 1
            return "A" if nab[0] % 2 == 1 else "B"

        c6_entries = {}
        gpost_ms = {}

        def make_gpost(e):
            def gpost_v(vector):
                rbw = ("RB0" if e % 2 == 0 else "RB1", 16 * (e // 2 + 1))
                src = rwb_bf[:, 1024 * (e % 2):1024 * (e % 2 + 1)]
                for j in range(6):
                    cx.wait(vector, gfills[(e, j)]["evac_ms"], rbw)
                    inst = vector.tensor_mul(q_t(j), q_t(j), src)
                    assert cx.inc(inst, "GV") == gpost_ms[(e, j)]
            return gpost_v

        for e in range(NE):
            for t in range(3):
                for dt in range(4):
                    j = 4 * t + dt
                    if j < 6:
                        def g_evac(scalar, psum, j=j):
                            return scalar.activation(q_t(j), psum[:, :],
                                                     AF.Silu)
                        gfills[(e, j)] = pipe7.add(
                            abkey(), gup_mm(e, t, dt), "S", g_evac,
                            pe_waits=[wms[f"g{e}_{t}"]],
                            tag=f"g{e}_{j}")
                        if e == 0 and j == 0:
                            c6_entries["logits"] = pipe7.add_custom(
                                pe=c6_logits_pe, v=c6_logits_v, mm_incs=1)
                        elif e == 0 and j == 1:
                            c6_entries["tr"] = pipe7.add_custom(
                                pe=c6_tr_pe, v=c6_tr_v, mm_incs=1)
                        elif e == 0 and j == 2:
                            c6_entries["soft"] = pipe7.add_custom(
                                s=c6_soft_s, v=c6_soft_v)
                        elif e == 0 and j == 4:
                            c6_entries["rt"] = pipe7.add_custom(
                                pe=c6_rt_pe, v=c6_rt_v, mm_incs=1)
                        if j == 5:
                            pipe7.add_custom(v=make_gpost(e))
                    else:
                        ju = j - 6

                        def u_evac(vector, psum, ju=ju):
                            return vector.tensor_mul(q_t(6 + ju), psum[:, :],
                                                     q_t(ju))
                        ufills[(e, ju)] = pipe7.add(
                            abkey(), gup_mm(e, t, dt), "V", u_evac,
                            pe_waits=[wms[f"g{e}_{t}"]],
                            evac_waits=[], tag=f"u{e}_{ju}")
            for hf in range(2):
                for dt in range(8):
                    gi = 8 * hf + dt

                    def d_evac(vector, psum, gi=gi):
                        return vector.tensor_add(res_t(gi), res_t(gi),
                                                 psum[:, :])
                    dfills[(e, gi)] = pipe7.add(
                        abkey(), dwn_mm(e, hf, dt), "V", d_evac,
                        pe_waits=[wms[f"d{e}_{hf}"]], tag=f"d{e}_{gi}")
        pipe7.plan()
        ms6["logits"] = c6_entries["logits"]["mm_end"]
        ms6["tr"] = c6_entries["tr"]["mm_end"]
        ms6["rt"] = c6_entries["rt"]["mm_end"]
        _g0 = cx.c["GV"]
        ms6["lg"] = _g0 + 1
        ms6["trv"] = _g0 + 2
        for b in range(8):
            ms6[f"rw{b}"] = _g0 + 3 + b
        ms6["rw_bf"] = _g0 + 11
        for e in range(NE):
            for j in range(6):
                gpost_ms[(e, j)] = _g0 + 11 + 6 * e + j + 1
        ms6["exp"] = cx.c["GS"] + 1
        for e in range(NE):
            for ju in range(6):
                ufills[(e, ju)]["evac_waits"].append(
                    ("GV", gpost_ms[(e, ju)]))
            for gi in range(NT):
                dfills[(e, gi)]["pe_waits"].append(
                    ufills[(e, 5)]["evac_ms"])
            if e > 0:
                for j in range(6):
                    gfills[(e, j)]["evac_waits"].append(
                        ufills[(e - 1, j)]["evac_ms"])
            ring_consumed[f"g{e}_0"] = gfills[(e, 3)]["mm_ms"]
            ring_consumed[f"g{e}_1"] = ufills[(e, 1)]["mm_ms"]
            ring_consumed[f"g{e}_2"] = ufills[(e, 5)]["mm_ms"]
            ring_consumed[f"d{e}_0"] = dfills[(e, 7)]["mm_ms"]
            ring_consumed[f"d{e}_1"] = dfills[(e, 15)]["mm_ms"]
        pipe7.emit(blk)

        def sync3(sync):
            cx.wait(sync, ("GV", ms6["rw_bf"]))
            inst = sync.dma_start(out=rw_dram, in_=rw_em_bf[:, :])
            cx.inc(inst, "RWS", 16)
            sync.wait_ge(sems["RWS"], 16)
            for e in range(NE):
                key = "RB0" if e % 2 == 0 else "RB1"
                if e >= 2:
                    cx.wait(sync, ("GV", gpost_ms[(e - 2, 5)]),
                            (key, 16 * (e // 2)))
                dst = rwb_bf[:, 1024 * (e % 2):1024 * (e % 2 + 1)]
                inst = sync.dma_start(
                    out=dst,
                    in_=rw_dram[e:e + 1, :].broadcast_to([128, 1024]))
                cx.inc(inst, key, 16)
        blk.sync(sync3)

        def sync_out(sync):
            for gi in range(NT):
                cx.wait(sync, dfills[(NE - 1, gi)]["evac_ms"])
                inst = sync.dma_start(out=out[128 * gi:128 * (gi + 1), :],
                                      in_=res_t(gi))
                cx.inc(inst, "ST", 16)
            cx.wait(sync, ("ST", cx.c["ST"]))
        blk.sync(sync_out)

        def gp(gpsimd):
            for name, kind, info, key in wplan:
                if kind == "cc":
                    cx.wait(gpsimd, XS_DONE)
                    inst = gpsimd.collective_compute(
                        "AllReduce", ALU.add,
                        replica_groups=[list(range(NCORES))],
                        ins=[cc_in[:, :].opt()], outs=[cc_out[:, :].opt()])
                    cx.inc(inst, "CC", 1)
                    continue
                if kind == "plain":
                    dst, src = info
                    inst = gpsimd.dma_start(out=dst, in_=src)
                else:
                    idx, width, src = info
                    if idx >= 2:
                        prev = ring_entries[idx - 2]
                        cx.wait(gpsimd, ("MM", ring_consumed[prev]),
                                (key, 16 * (idx // 2)))
                    inst = gpsimd.dma_start(out=slot_ap(idx, width), in_=src)
                got = cx.inc(inst, key, 16)
                assert (key, got) == wms[name], (name, key, got, wms[name])
        blk.gpsimd(gp)

    return nc


def _pack_weights(inputs):
    f = np.float32

    def attn(w):
        return np.ascontiguousarray(
            np.asarray(w, f).reshape(16, 128, 4, 512).transpose(2, 1, 0, 3)
        ).reshape(4, 128, 8192)

    d = {}
    d["wq"] = attn(inputs["L0_attn_Wq"])
    d["wk"] = attn(inputs["L0_attn_Wk"])
    d["wv"] = attn(inputs["L0_attn_Wv"])
    d["wo"] = attn(inputs["L0_attn_Wo"])
    d["boc"] = np.ascontiguousarray(
        np.asarray(inputs["L0_attn_bo"], f).reshape(16, 128).T)
    d["gatep"] = np.ascontiguousarray(
        np.asarray(inputs["L0_ffn_gate"], f).reshape(16, 128, 32)
        .transpose(1, 0, 2)).reshape(128, 512)
    g = np.asarray(inputs["L0_ffn_gate_up_proj"], f)
    d["gup"] = np.ascontiguousarray(
        g.reshape(32, 16, 128, 3, 512).transpose(0, 3, 2, 1, 4)
    ).reshape(32, 3, 128, 8192)
    dn = np.asarray(inputs["L0_ffn_down_proj"], f)
    d["dwn"] = np.ascontiguousarray(
        dn.reshape(32, 6, 128, 2, 1024).transpose(0, 3, 2, 1, 4)
    ).reshape(32, 2, 128, 6144)
    d["cid"] = np.eye(128, dtype=f)
    pat = np.zeros((128, 2), f)
    pat[:64, 0] = 1.0
    pat[64:, 1] = 1.0
    d["cpat"] = pat
    d["cpatT"] = np.ascontiguousarray(pat.T)
    d["conec"] = np.ones((128, 1), f)
    d["coner"] = np.ones((1, 128), f)
    return d


def kernel(L0_attn_hidden_states, L0_attn_Wq, L0_attn_Wk, L0_attn_Wv,
           L0_attn_Wo, L0_attn_bo, L0_ffn_gate, L0_ffn_gate_up_proj,
           L0_ffn_down_proj):
    shared = _pack_weights(dict(
        L0_attn_Wq=L0_attn_Wq, L0_attn_Wk=L0_attn_Wk, L0_attn_Wv=L0_attn_Wv,
        L0_attn_Wo=L0_attn_Wo, L0_attn_bo=L0_attn_bo, L0_ffn_gate=L0_ffn_gate,
        L0_ffn_gate_up_proj=L0_ffn_gate_up_proj,
        L0_ffn_down_proj=L0_ffn_down_proj))
    x = np.asarray(L0_attn_hidden_states, np.float32).reshape(8192, 2048)
    in_maps = []
    for c in range(NCORES):
        xc = np.ascontiguousarray(x[1024 * c:1024 * (c + 1), :].T)
        m = dict(shared)
        m["xT"] = xc.reshape(16, 128, 1024)
        in_maps.append(m)
    nc = build_nc()
    res = run_bass_kernel_spmd(nc, in_maps, core_ids=list(range(NCORES)))
    outp = np.empty((8192, 2048), np.float32)
    for c in range(NCORES):
        outp[1024 * c:1024 * (c + 1), :] = res.results[c]["out"].T
    return outp.reshape(1, 8192, 2048)
